# version 31
# baseline (speedup 1.0000x reference)
import contextlib
import numpy as np
import ml_dtypes
import concourse.bass as bass
import concourse.mybir as mybir
from concourse.bass_utils import run_bass_kernel_spmd

F32 = mybir.dt.float32
BF16 = mybir.dt.bfloat16
AF = mybir.ActivationFunctionType
ALU = mybir.AluOpType

D = 1024
FF = 4096
E = 2048
NST = 16
TC = 512
ND = D // 128
NF = FF // 128
NE = E // 128
EPS = 1e-5
INV_D = 1.0 / D
ENGINES = ("sync", "tensor", "scalar", "vector", "gpsimd")
GROUPS = [list(range(4 * i, 4 * i + 4)) for i in range(4)]


class Sem:
    __slots__ = ("h", "v", "name")


class Sched:
    def __init__(self, nc, st):
        self.nc = nc
        self.st = st
        self.items = {e: [] for e in ENGINES}

    def sem(self, name):
        s = Sem()
        s.h = self.st.enter_context(self.nc.semaphore(name))
        s.v = 0
        s.name = name
        return s

    def add(self, eng, fn, waits=(), inc=None, amt=1):
        w = [x for x in waits if x is not None]
        self.items[eng].append((tuple(w), fn, inc, amt))
        if inc is not None:
            inc.v += amt
            return (inc, inc.v)
        return None

    def emit(self, block):
        for eng in ENGINES:
            items = self.items[eng]
            if not items:
                continue
            section = getattr(block, eng)

            def make(items=items):
                def _(engobj):
                    last = {}
                    for waits, fn, inc, amt in items:
                        for (s, v) in waits:
                            if v > last.get(id(s), 0):
                                engobj.wait_ge(s.h, v)
                                last[id(s)] = v
                        inst = fn(engobj)
                        if inc is not None:
                            inst.then_inc(inc.h, amt)
                return _
            section(make())


def build_nc(L):
    NCH = L // TC
    nc = bass.Bass("TRN2", target_bir_lowering=False,
                   detect_race_conditions=False)

    di = lambda n, s, d: nc.dram_tensor(n, s, d, kind="ExternalInput").ap()
    x_h = di("x", [D, L], F32)
    w1_h = di("w1blk", [NF, 8, 128, 128], F32)
    w2_h = di("w2blk", [ND, 32, 128, 128], F32)
    win_h = di("winblk", [NF, 8, 128, 128], F32)
    wout_h = di("woutblk", [ND, 16, 128, 128], F32)
    wxp_h = di("wxpblk", [16, 128, 96], F32)
    wdt_h = di("wdtblk", [16, 64, 128], F32)
    cdiag_h = di("cdiag", [NE, 4, 128, 128], F32)
    ddiag_h = di("ddiag", [128, NE * 128], F32)
    alog_h = di("alogc", [128, NE * NST], F32)
    b1c_h = di("b1c", [128, NF], F32)
    b2c_h = di("b2c", [128, ND], F32)
    cbc_h = di("cbc", [128, NE], F32)
    bdtc_h = di("bdtc", [128, NE], F32)
    lnwc_h = di("lnwc", [128, ND], F32)
    lnbc_h = di("lnbc", [128, ND], F32)
    eye_h = di("eye128", [128, 128], BF16)
    onescol_h = di("onescol", [128, 1], F32)
    onescolb_h = di("onescolb", [128, 1], BF16)

    out_h = nc.dram_tensor("out", [D, L], F32, kind="ExternalOutput").ap()

    ds = lambda n, s, d: nc.dram_tensor(n, s, d, kind="Internal").ap()
    w1s = ds("w1s", [NF, 8, 128, 128], BF16)
    w2s = ds("w2s", [ND, 32, 128, 128], BF16)
    wins = ds("wins", [NF, 8, 128, 128], BF16)
    wouts = ds("wouts", [ND, 16, 128, 128], BF16)
    cds = ds("cds", [NE, 4, 128, 128], BF16)
    bc_dram = ds("bc_dram", [32, TC], BF16)
    x1_d = ds("x1_d", [128, ND * TC], F32)
    x2_d = ds("x2_d", [128, ND * TC], F32)
    xn_d = ds("xn_d", [128, ND * TC], BF16)
    z_d = ds("z_d", [128, NE * TC], BF16)
    u_d = ds("u_d", [128, NE * TC], BF16)

    def wblk(ap, i, kc):
        return bass.AP(ap.tensor, i * kc * 16384,
                       [[128, 128], [16384, kc], [1, 128]])

    def _xio(ap, c):
        return bass.AP(ap.tensor, c * TC, [[L, 128], [128 * L, ND], [1, TC]])

    def b_bcpair(n):
        return bass.AP(bc_dram.tensor, 2 * n * TC, [[0, 128], [TC, 2], [1, TC]])

    with contextlib.ExitStack() as st:
        sb = lambda n, s, d: st.enter_context(nc.sbuf_tensor(n, s, d))
        X = sb("X", [128, ND * TC], F32)
        dA2 = sb("dA2", [128, 3 * TC], F32)
        hstate = sb("hstate", [128, NE * NST], F32)
        aneg = sb("aneg", [128, NE * NST], F32)
        thrs = sb("thrs", [128, 4], F32)
        absacc = sb("absacc", [128, 64], F32)
        repRS = sb("repRS", [128, TC], F32)
        repMR = sb("repMR", [128, TC], BF16)
        sp = sb("sp", [128, 2 * TC], F32)
        rowf = sb("rowf", [1, 6 * TC], F32)
        xsl = sb("xsl", [128, 2 * TC], F32)
        zq = sb("zq", [128, 2 * TC], F32)
        xnsl = sb("xnsl", [128, 2 * TC], BF16)
        b1c = sb("b1c_s", [128, NF], F32)
        b2c = sb("b2c_s", [128, ND], F32)
        cbc = sb("cbc_s", [128, NE], F32)
        bdtc = sb("bdtc_s", [128, NE], F32)
        lnwc = sb("lnwc_s", [128, ND], F32)
        lnbc = sb("lnbc_s", [128, ND], F32)
        onescol = sb("onescol_s", [128, 1], F32)
        onescolb = sb("onescolb_s", [128, 1], BF16)
        ones1x128 = sb("ones1x128_s", [1, 128], F32)
        consts = sb("consts_s", [128, 2], F32)

        SCR = sb("SCR", [128, 2 * ND * TC], BF16)
        ARENA = sb("ARENA", [128, NF * TC], BF16)
        dtbf = sb("dtbf", [128, NE * TC], BF16)
        XSYG = sb("XSYG", [128, NE * (TC + 3)], BF16)
        dtu = sb("dtu", [128, NE * TC], BF16)
        wbuf = sb("wbuf", [128, 8192], BF16)
        BC = sb("BC", [128, 10 * TC], BF16)
        dBu = sb("dBu", [128, 4 * TC], BF16)
        hbuf = sb("hbuf", [128, 4 * TC], BF16)
        tmpb = sb("tmpb", [128, 4 * TC], BF16)
        xsq = sb("xsq", [128, 2 * TC], BF16)
        cdsl = sb("cdsl", [128, 2 * 512], BF16)
        ddiag = sb("ddiag_s", [128, NE * 128], BF16)
        wxp = sb("wxp_s", [128, 16 * 96], BF16)
        wdt = sb("wdt_s", [64, 16 * 128], BF16)
        eye = sb("eye_s", [128, 128], BF16)
        bcbf = sb("bcbf", [32, TC], BF16)
        dtrbf = sb("dtrbf", [64, TC], BF16)
        halos = sb("halos", [128, NE * 3], BF16)
        PS = [st.enter_context(nc.psum_tensor(f"ps{i}", [128, TC], F32))
              for i in range(8)]

        sch = Sched(nc, st)
        sPE = sch.sem("sPE")
        sACT = sch.sem("sACT")
        sDVE = sch.sem("sDVE")
        sGP = sch.sem("sGP")
        semX = sch.sem("semX")
        semO = sch.sem("semO")
        semW = [sch.sem(f"semW{i}") for i in range(4)]
        semB = [sch.sem(f"semB{i}") for i in range(5)]
        semZS = sch.sem("semZS")
        zh_state = {"mm": None}
        semC = [sch.sem(f"semC{i}") for i in range(2)]
        semSL = [sch.sem(f"semSL{i}") for i in range(2)]
        semNL = [sch.sem(f"semNL{i}") for i in range(2)]
        semZL = [sch.sem(f"semZL{i}") for i in range(2)]
        semSP = sch.sem("semSP")
        semSCR = sch.sem("semSCR")
        semS = [sch.sem(f"semS{i}") for i in range(4)]
        semMISC = sch.sem("semMISC")

        def fence(s):
            return (s, s.v)

        ps_free = {i: None for i in range(8)}
        wbuf_deps = []
        sp_war = {0: None, 1: None}
        bcw_prev = [None]
        xsl_war = {0: None, 1: None}
        cdsl_war = {0: None, 1: None}
        xnsl_war = {0: None, 1: None}
        zsl_war = {0: None, 1: None}
        sq_war = {0: None, 1: None}
        sq_idx = [0]

        def wbuf_claim(lo, hi):
            w, keep = [], []
            for ent in wbuf_deps:
                if ent[0] < hi and lo < ent[1]:
                    w.append(ent[2])
                else:
                    keep.append(ent)
            wbuf_deps[:] = keep
            return w

        def wbuf_use(lo, hi, dep):
            wbuf_deps.append((lo, hi, dep))

        def pw(bank, extra=()):
            w = list(extra)
            if ps_free[bank] is not None:
                w.append(ps_free[bank])
            return w

        def dma(dst, src, sem, waits=()):
            sch.add("sync", lambda e, d=dst, s=src: e.dma_start(d, s),
                    waits=waits, inc=sem, amt=16)
            return fence(sem)

        for dst, src in ((b1c, b1c_h), (b2c, b2c_h), (cbc, cbc_h), (bdtc, bdtc_h),
                         (lnwc, lnwc_h), (lnbc, lnbc_h), (onescol, onescol_h),
                         (onescolb, onescolb_h), (eye, eye_h)):
            dma(dst[:, :], src, semMISC)
        sch.add("vector", lambda e: e.memset(hstate[:, :], 0.0), inc=sDVE)
        sch.add("vector", lambda e: e.memset(halos[:, :], 0.0), inc=sDVE)
        sch.add("vector", lambda e: e.memset(consts[:, 0:1], EPS), inc=sDVE)
        sch.add("vector", lambda e: e.memset(consts[:, 1:2], 1.0), inc=sDVE)
        sch.add("vector", lambda e: e.memset(ones1x128[0:1, :], 1.0), inc=sDVE)

        def w_mean_pass(v_src, nblk, width, acc_off):
            acts = []
            ns = 4096 // width
            for i in range(nblk):
                par = i % ns
                w = [acts[i - ns]] if i >= ns else [(sDVE, sDVE.v)]
                ld = dma(X[:, par * width: par * width + width], v_src(i),
                         semS[par], waits=w)
                a = sch.add("scalar", lambda e, p=par, wd=width, i=i: e.activation(
                    SCR[:, p * wd: p * wd + wd], X[:, p * wd: p * wd + wd],
                    AF.Abs, bias=0.0, scale=1.0,
                    accum_out=absacc[:, acc_off + i: acc_off + i + 1]),
                    waits=[ld], inc=sACT)
                acts.append(a)
            return acts[-1]

        def scale_math(off, n, ti, nelem, prev):
            v = sch.add("vector", lambda e, off=off, n=n: e.tensor_reduce(
                absacc[:, 63:64], absacc[:, off:off + n],
                mybir.AxisListType.X, ALU.add), waits=[prev], inc=sDVE)
            b = 6
            m = sch.add("tensor", lambda e: e.matmul(
                PS[b][0:1, 0:1], onescol[:, 0:1], absacc[:, 63:64],
                start=True, stop=True),
                waits=pw(b, [v, fence(semMISC)]), inc=sPE)
            a = sch.add("scalar", lambda e, nelem=nelem: e.activation(
                rowf[0:1, 0:1], PS[b][0:1, 0:1], AF.Copy, bias=0.0,
                scale=1.0 / nelem), waits=[m], inc=sACT)
            ps_free[b] = a
            v = sch.add("vector", lambda e: e.tensor_scalar_max(
                rowf[0:1, 0:1], rowf[0:1, 0:1], 1e-8), waits=[a], inc=sDVE)
            a = sch.add("scalar", lambda e: e.mul(
                rowf[0:1, 0:1], rowf[0:1, 0:1], 0.5), waits=[v], inc=sACT)
            m = sch.add("tensor", lambda e: e.matmul(
                PS[b][:, 0:1], ones1x128[0:1, :], rowf[0:1, 0:1],
                start=True, stop=True), waits=pw(b, [a]), inc=sPE)
            a = sch.add("scalar", lambda e, ti=ti: e.copy(
                thrs[:, ti:ti + 1], PS[b][:, 0:1]), waits=[m], inc=sACT)
            ps_free[b] = a
            return sch.add("scalar", lambda e, ti=ti: e.mul(
                thrs[:, ti + 1:ti + 2], thrs[:, ti:ti + 1], -1.0), inc=sACT)

        def tern_pass(v_src, v_dst, nblk, width, ti, thr_ready):
            dves, outs = [], []
            ns = 4096 // width
            for i in range(nblk):
                par = i % ns
                w = [dves[i - ns]] if i >= ns else [thr_ready]
                ld = dma(X[:, par * width: par * width + width], v_src(i),
                         semS[par], waits=w)

                def tern(e, p=par, wd=width, ti=ti):
                    xs = X[:, p * wd: p * wd + wd]
                    nc.vector.tensor_scalar(SCR[:, p * wd: p * wd + wd], xs,
                                            thrs[:, ti:ti + 1], None, ALU.is_gt)
                    nc.vector.tensor_scalar(
                        SCR[:, 4096 + p * wd: 4096 + p * wd + wd], xs,
                        thrs[:, ti + 1:ti + 2], None, ALU.is_lt)
                    return nc.vector.tensor_sub(
                        ARENA[:, p * wd: p * wd + wd],
                        SCR[:, p * wd: p * wd + wd],
                        SCR[:, 4096 + p * wd: 4096 + p * wd + wd])
                w2 = [ld] + ([outs[i - ns]] if i >= ns else [])
                v = sch.add("vector", tern, waits=w2, inc=sDVE)
                dves.append(v)
                outs.append(dma(v_dst(i), ARENA[:, par * width: par * width + width],
                                semSCR, waits=[v]))
            return outs[-1]

        def cast_stream(v_src, v_dst, nblk, width):
            dves, outs = [], []
            ns = 4096 // width
            for i in range(nblk):
                par = i % ns
                w = [dves[i - ns]] if i >= ns else [(sDVE, sDVE.v)]
                ld = dma(X[:, par * width: par * width + width], v_src(i),
                         semS[par], waits=w)
                w2 = [ld] + ([outs[i - ns]] if i >= ns else [])
                v = sch.add("vector", lambda e, p=par, wd=width: e.tensor_copy(
                    ARENA[:, p * wd: p * wd + wd],
                    X[:, p * wd: p * wd + wd]), waits=w2, inc=sDVE)
                dves.append(v)
                outs.append(dma(v_dst(i), ARENA[:, par * width: par * width + width],
                                semSCR, waits=[v]))
            return outs[-1]

        a = w_mean_pass(lambda i: wblk(w1_h, i, 8), 32, 1024, 0)
        t1r = scale_math(0, 32, 0, FF * D, a)
        w1s_ready = tern_pass(lambda i: wblk(w1_h, i, 8),
                              lambda i: wblk(w1s, i, 8), 32, 1024, 0, t1r)
        a = w_mean_pass(lambda i: wblk(w2_h, i, 16), 16, 2048, 32)
        t2r = scale_math(32, 16, 2, D * FF, a)
        w2s_ready = tern_pass(lambda i: wblk(w2_h, i, 16),
                              lambda i: wblk(w2s, i, 16), 16, 2048, 2, t2r)
        wins_ready = cast_stream(lambda i: wblk(win_h, i, 8),
                                 lambda i: wblk(wins, i, 8), 32, 1024)
        wouts_ready = cast_stream(lambda i: wblk(wout_h, i, 16),
                                  lambda i: wblk(wouts, i, 16), 8, 2048)
        cds_ready = cast_stream(lambda i: wblk(cdiag_h, i, 8),
                                lambda i: wblk(cds, i, 8), 8, 1024)

        def resident_cast(src_ap, dst_ap, width, rows=128):
            ld = dma(X[0:rows, 0:width], src_ap, semS[0], waits=[(sDVE, sDVE.v)])
            return sch.add("vector", lambda e: e.tensor_copy(
                dst_ap, X[0:rows, 0:width]), waits=[ld], inc=sDVE)

        resident_cast(ddiag_h, ddiag[:, :], 2048)
        resident_cast(
            bass.AP(wxp_h.tensor, 0, [[96, 128], [128 * 96, 16], [1, 96]]),
            wxp[:, :], 16 * 96)
        resident_cast(
            bass.AP(wdt_h.tensor, 0, [[128, 64], [64 * 128, 16], [1, 128]]),
            wdt[:, :], 16 * 128, rows=64)
        ld = dma(X[:, 0:256], alog_h, semS[0], waits=[(sDVE, sDVE.v)])
        a = sch.add("scalar", lambda e: e.activation(
            aneg[:, :], X[:, 0:256], AF.Exp, bias=0.0, scale=1.0),
            waits=[ld], inc=sACT)
        sch.add("vector", lambda e: e.tensor_scalar_mul(
            aneg[:, :], aneg[:, :], -1.0), waits=[a], inc=sDVE)
        setup_v = (sDVE, sDVE.v)
        setup_a = (sACT, sACT.v)

        def r_(i):
            return rowf[0:1, i * TC:(i + 1) * TC]

        def xn_sl(k):
            return SCR[:, ND * TC + k * TC: ND * TC + (k + 1) * TC]

        def u_sl(k):
            return ARENA[:, 8192 + k * TC: 8192 + (k + 1) * TC]

        def emit_stats_d(d, src_ready, src=None, bx=4, bq=5, idx=None):
            idx = d if idx is None else idx
            src = X[:, d * TC:(d + 1) * TC] if src is None else src
            cast = sch.add("vector", lambda e, src=src, d=d: e.tensor_copy(
                SCR[:, d * TC:(d + 1) * TC], src),
                waits=[src_ready] if src_ready is not None else [], inc=sDVE)
            w = pw(bx, [cast, fence(semMISC)]) if idx == 0 else [cast]
            sch.add("tensor", lambda e, d=d, bx=bx, idx=idx: e.matmul(
                PS[bx][0:1, :], onescolb[:, 0:1], SCR[:, d * TC:(d + 1) * TC],
                start=(idx == 0), stop=(idx == ND - 1)), waits=w, inc=sPE)
            par = sq_idx[0] % 2
            sq_idx[0] += 1
            xslc = xsq[:, par * TC:(par + 1) * TC]
            wsq = [src_ready] if src_ready is not None else []
            if sq_war[par] is not None:
                wsq = wsq + [sq_war[par]]
            sq = sch.add("scalar", lambda e, src=src, xslc=xslc: e.activation(
                xslc, src, AF.Square, bias=0.0, scale=1.0),
                waits=wsq, inc=sACT)
            w2 = pw(bq, [sq]) if idx == 0 else [sq]
            mm = sch.add("tensor", lambda e, bq=bq, xslc=xslc, idx=idx: e.matmul(
                PS[bq][0:1, :], onescolb[:, 0:1], xslc,
                start=(idx == 0), stop=(idx == ND - 1)), waits=w2, inc=sPE)
            sq_war[par] = mm
            return mm

        def emit_stats_evac(mm):
            a = sch.add("scalar", lambda e: e.copy(r_(4), PS[6][0:1, :]),
                        waits=[mm], inc=sACT)
            a = sch.add("scalar", lambda e: e.copy(r_(5), PS[7][0:1, :]),
                        inc=sACT)
            ps_free[6] = a
            ps_free[7] = a
            return a

        def emit_ln_rows(mamba, from_rows=None, stats_dep=None):
            if from_rows is None:
                sch.add("scalar", lambda e: e.activation(
                    r_(0), PS[4][0:1, :], AF.Copy, bias=0.0, scale=INV_D),
                    waits=[(sPE, sPE.v)] if stats_dep is None else [stats_dep],
                    inc=sACT)
                a = sch.add("scalar", lambda e: e.activation(
                    r_(1), PS[5][0:1, :], AF.Copy, bias=0.0, scale=INV_D),
                    inc=sACT)
                ps_free[4] = a
                ps_free[5] = a
            else:
                sch.add("scalar", lambda e: e.activation(
                    r_(0), r_(4), AF.Copy, bias=0.0, scale=INV_D),
                    waits=[stats_dep], inc=sACT)
                a = sch.add("scalar", lambda e: e.activation(
                    r_(1), r_(5), AF.Copy, bias=0.0, scale=INV_D), inc=sACT)
            sch.add("scalar", lambda e: e.square(r_(2), r_(0)), inc=sACT)
            v = sch.add("vector", lambda e: e.tensor_sub(r_(1), r_(1), r_(2)),
                        waits=[(sACT, sACT.v)], inc=sDVE)
            a = sch.add("scalar", lambda e: e.activation(
                r_(2), r_(1), AF.Ln, bias=consts[0:1, 0:1], scale=1.0),
                waits=[v], inc=sACT)
            a = sch.add("scalar", lambda e: e.activation(
                r_(3), r_(2), AF.Exp, bias=0.0, scale=-0.5), inc=sACT)
            if mamba:
                v = sch.add("vector", lambda e: e.tensor_scalar_add(
                    r_(2), r_(1), EPS), waits=[a], inc=sDVE)
                v = sch.add("vector", lambda e: e.reciprocal(r_(2), r_(2)),
                            inc=sDVE)
                v = sch.add("vector", lambda e: e.tensor_mul(r_(2), r_(1), r_(2)),
                            inc=sDVE)
                a = sch.add("scalar", lambda e: e.activation(
                    r_(2), r_(2), AF.Ln, bias=consts[0:1, 0:1], scale=1.0),
                    waits=[v], inc=sACT)
                a = sch.add("scalar", lambda e: e.activation(
                    r_(2), r_(2), AF.Exp, bias=0.0, scale=-0.5), inc=sACT)
                v = sch.add("vector", lambda e: e.tensor_mul(r_(3), r_(3), r_(2)),
                            waits=[a], inc=sDVE)
            v = sch.add("vector", lambda e: e.tensor_mul(r_(2), r_(0), r_(3)),
                        waits=[(sACT, sACT.v)], inc=sDVE)
            m1 = sch.add("tensor", lambda e: e.matmul(
                PS[4][:, :], ones1x128[0:1, :], r_(3), start=True, stop=True),
                waits=pw(4, [v]), inc=sPE)
            m2 = sch.add("tensor", lambda e: e.matmul(
                PS[5][:, :], ones1x128[0:1, :], r_(2), start=True, stop=True),
                waits=pw(5, [v]), inc=sPE)
            rs = sch.add("scalar", lambda e: e.copy(repRS[:, :], PS[4][:, :]),
                         waits=[m1, (sDVE, sDVE.v)], inc=sACT)
            ps_free[4] = rs
            mr = sch.add("scalar", lambda e: e.copy(repMR[:, :], PS[5][:, :]),
                         waits=[m2], inc=sACT)
            ps_free[5] = mr
            return rs, mr

        def emit_ln_apply_X(mamba, rs, mr, extra=None):
            last = None
            for d in range(ND):
                sl = slice(d * TC, (d + 1) * TC)
                sch.add("vector", lambda e, sl=sl: e.tensor_mul(
                    SCR[:, sl], X[:, sl], repRS[:, :]),
                    waits=([rs, extra] if d == 0 else []), inc=sDVE)
                last = sch.add("vector", lambda e, d=d, sl=sl: e.tensor_sub(
                    xn_sl(d), SCR[:, sl], repMR[:, :]),
                    waits=[mr] if d == 0 else [], inc=sDVE)
                if mamba:
                    last = sch.add("vector", lambda e, d=d: e.tensor_scalar(
                        xn_sl(d), xn_sl(d), lnwc[:, d:d + 1], lnbc[:, d:d + 1],
                        ALU.mult, ALU.add), inc=sDVE)
            return last

        def wload(view, slot, width, ready_first, sem_idx=None):
            lo = slot * 2048
            w = ([ready_first] if ready_first is not None else []) \
                + wbuf_claim(lo, lo + width)
            f = dma(wbuf[:, lo:lo + width], view, semW[slot if sem_idx is None
                                                       else sem_idx], waits=w)
            return f, lo

        def gen_zh(c, prev_gates_f, xn_f):
            zpend = []
            last_mm = None
            for jp, blk in enumerate(range(8, 16)):
                slot = jp % 4
                ld, lo = wload(wblk(wins, blk, 16), slot, 2048,
                               wins_ready if jp < 4 else None)
                m = None
                for h in range(2):
                    b = 4 + (2 * jp + h) % 4

                    def mmg(e, lo=lo, h=h, b=b):
                        last = None
                        for k in range(8):
                            o = lo + (h * 8 + k) * 128
                            last = nc.tensor.matmul(
                                PS[b][:, :], wbuf[:, o: o + 128], xn_sl(k),
                                start=(k == 0), stop=(k == 7))
                        return last
                    m = sch.add("tensor", mmg, waits=pw(b, [ld, xn_f]),
                                inc=sPE)
                    last_mm = m
                    zpend.append((2 * blk + h - 16, b, m))
                    while len(zpend) > 2:
                        ei2, b2, m2 = zpend.pop(0)
                        a = sch.add("scalar", lambda e, ei=ei2, b=b2: e.copy(
                            ARENA[:, ei * TC:(ei + 1) * TC], PS[b][:, :]),
                            waits=[m2], inc=sACT)
                        ps_free[b2] = a
                wbuf_use(lo, lo + 2048, m)
                zh_state["mm"] = last_mm
                yield
            for ei2, b2, m2 in zpend:
                a = sch.add("scalar", lambda e, ei=ei2, b=b2: e.copy(
                    ARENA[:, ei * TC:(ei + 1) * TC], PS[b][:, :]),
                    waits=[m2], inc=sACT)
                ps_free[b2] = a
            w = [(sACT, sACT.v)]
            if prev_gates_f is not None:
                w.append(prev_gates_f)
            dma(bass.AP(z_d.tensor, 0, [[NE * TC, 128], [1, NE * TC]]),
                ARENA[:, 0:NE * TC], semZS, waits=w)
            zh_state["mm"] = last_mm

        def gen_ffn_A(c, x_ready, wout_pref=None, mstate=None):
            rs, mr = emit_ln_rows(False, stats_dep=None)
            yield
            xn = emit_ln_apply_X(False, rs, mr, extra=fence(semSP))
            yield
            gelu_last = None
            gpend = []
            for fp in range(NF // 2):
                slot = fp % 4
                ld, lo = wload(wblk(w1s, fp, 16), slot, 2048,
                               w1s_ready if fp < 4 else None)
                m = None
                for h in range(2):
                    f = 2 * fp + h
                    b = 4 + (f % 4)

                    def mmg(e, lo=lo, h=h, b=b):
                        last = None
                        for k in range(8):
                            o = lo + (h * 8 + k) * 128
                            last = nc.tensor.matmul(
                                PS[b][:, :], wbuf[:, o: o + 128], xn_sl(k),
                                start=(k == 0), stop=(k == 7))
                        return last
                    m = sch.add("tensor", mmg,
                                waits=pw(b, [ld, xn]), inc=sPE)
                    gpend.append((f, b, m))
                    while len(gpend) > 3:
                        f2, b2, m2 = gpend.pop(0)
                        gelu_last = sch.add(
                            "scalar", lambda e, f=f2, b=b2: e.activation(
                                ARENA[:, f * TC:(f + 1) * TC], PS[b][:, :],
                                AF.Gelu, bias=b1c[:, f:f + 1], scale=1.0),
                            waits=[m2], inc=sACT)
                        ps_free[b2] = gelu_last
                wbuf_use(lo, lo + 2048, m)
                if fp % 2 == 1:
                    yield
            for f2, b2, m2 in gpend:
                gelu_last = sch.add("scalar", lambda e, f=f2, b=b2: e.activation(
                    ARENA[:, f * TC:(f + 1) * TC], PS[b][:, :], AF.Gelu,
                    bias=b1c[:, f:f + 1], scale=1.0), waits=[m2], inc=sACT)
                ps_free[b2] = gelu_last
            gpend = []
            yield
            pend = None

            spend = []

            def do_tail(d, a):
                par = d % 2
                spsl = sp[:, par * TC: par * TC + TC]
                res = sch.add("vector",
                              lambda e, spsl=spsl: e.scalar_tensor_tensor(
                                  X[:, d * TC:(d + 1) * TC], spsl, 0.5,
                                  X[:, d * TC:(d + 1) * TC],
                                  ALU.mult, ALU.add), waits=[a], inc=sDVE)
                sp_war[par] = res
                spend.append((d, res))
                if len(spend) > 1:
                    emit_stats_d(*spend.pop(0))
                gen_ffn_A.res = res
                if d == ND - 1:
                    while spend:
                        emit_stats_d(*spend.pop(0))

            mm_q = []

            def do_evac(d, b, m):
                par = d % 2
                spsl = sp[:, par * TC: par * TC + TC]
                we = [m] + ([sp_war[par]] if sp_war[par] is not None else [])
                a = sch.add("scalar", lambda e, d=d, b=b, spsl=spsl: e.activation(
                    spsl, PS[b][:, :], AF.Identity, bias=b2c[:, d:d + 1],
                    scale=1.0), waits=we, inc=sACT)
                ps_free[b] = a
                return (d, a)

            for d in range(ND):
                ld0, lo0 = wload(
                    bass.AP(w2s.tensor, (d * 32) * 16384,
                            [[128, 128], [16384, 16], [1, 128]]),
                    (2 * d) % 4, 2048, w2s_ready if d < 2 else None)
                ld1, lo1 = wload(
                    bass.AP(w2s.tensor, (d * 32 + 16) * 16384,
                            [[128, 128], [16384, 16], [1, 128]]),
                    (2 * d + 1) % 4, 2048, w2s_ready if d < 2 else None)
                b = 6 + (d % 2)

                if len(mm_q) > 1:
                    if pend is not None:
                        do_tail(*pend)
                    pend = do_evac(*mm_q.pop(0))

                def mmg2(e, lo0=lo0, lo1=lo1, b=b):
                    last = None
                    for k in range(32):
                        lo = lo0 if k < 16 else lo1
                        o = lo + (k % 16) * 128
                        last = nc.tensor.matmul(
                            PS[b][:, :], wbuf[:, o: o + 128],
                            ARENA[:, k * TC:(k + 1) * TC],
                            start=(k == 0), stop=(k == 31))
                    return last
                m = sch.add("tensor", mmg2, waits=pw(b, [ld0, ld1, gelu_last]),
                            inc=sPE)
                wbuf_use(lo0, lo0 + 2048, m)
                wbuf_use(lo1, lo1 + 2048, m)
                mm_q.append((d, b, m))
                yield
            while mm_q:
                if pend is not None:
                    do_tail(*pend)
                pend = do_evac(*mm_q.pop(0))
            do_tail(*pend)
            if wout_pref is not None and "done" not in wout_pref:
                for d2 in range(2):
                    wout_pref[d2] = wload(wblk(wouts, d2, 16), d2 % 2, 2048,
                                          wouts_ready)
                if mstate is not None:
                    for d2 in range(2):
                        par = d2 % 2
                        w = [mstate["sp_xn"]]
                        if xnsl_war[par] is not None:
                            w.append(xnsl_war[par])
                        wout_pref[("ldn", d2)] = dma(
                            xnsl[:, par * TC:(par + 1) * TC],
                            bass.AP(xn_d.tensor, d2 * TC,
                                    [[ND * TC, 128], [1, TC]]),
                            semNL[par], waits=w)
                        xnsl_war[par] = None
                        w = [mstate["sp_x1"]]
                        if xsl_war[par] is not None:
                            w.append(xsl_war[par])
                        wout_pref[("lds", d2)] = dma(
                            xsl[:, par * TC:(par + 1) * TC],
                            bass.AP(x1_d.tensor, d2 * TC,
                                    [[ND * TC, 128], [1, TC]]),
                            semSL[par], waits=w)
                        xsl_war[par] = None

        def gen_ffn_C(c, stats_dep, pre_lds=None, zh=None):
            rs, mr = emit_ln_rows(False, from_rows=True, stats_dep=stats_dep)
            yield
            lds = dict(pre_lds) if pre_lds else {}
            for d in range(2):
                if d in lds:
                    continue
                w = [rs] if d == 0 else []
                if xsl_war[d % 2] is not None:
                    w.append(xsl_war[d % 2])
                lds[d] = dma(xsl[:, (d % 2) * TC:(d % 2 + 1) * TC],
                             bass.AP(x2_d.tensor, d * TC,
                                     [[ND * TC, 128], [1, TC]]),
                             semSL[d % 2], waits=w)
            xn = None
            for d in range(ND):
                par = d % 2
                wmul = [lds[d]] + ([rs, fence(semSP)] if d == 0 else [])
                if d == 0 and zh is not None and zh["mm"] is not None:
                    wmul.append(zh["mm"])
                sch.add("vector", lambda e, d=d, par=par: e.tensor_mul(
                    SCR[:, d * TC:(d + 1) * TC],
                    xsl[:, par * TC:(par + 1) * TC], repRS[:, :]),
                    waits=wmul, inc=sDVE)
                xn = sch.add("vector", lambda e, d=d: e.tensor_sub(
                    xn_sl(d), SCR[:, d * TC:(d + 1) * TC], repMR[:, :]),
                    waits=[mr] if d == 0 else [], inc=sDVE)
                xsl_war[par] = xn
                if d + 2 < ND:
                    lds[d + 2] = dma(xsl[:, par * TC:(par + 1) * TC],
                                     bass.AP(x2_d.tensor, (d + 2) * TC,
                                             [[ND * TC, 128], [1, TC]]),
                                     semSL[par], waits=[xn])
                if d % 2 == 1:
                    yield
            gelu_last = None
            gpend = []
            zspill_w = [fence(semZS)] if zh is not None else []
            for fp in range(NF // 2):
                slot = fp % 4
                ld, lo = wload(wblk(w1s, fp, 16), slot, 2048, None)
                m = None
                for h in range(2):
                    f = 2 * fp + h
                    b = 4 + (f % 4)

                    def mmg(e, lo=lo, h=h, b=b):
                        last = None
                        for k in range(8):
                            o = lo + (h * 8 + k) * 128
                            last = nc.tensor.matmul(
                                PS[b][:, :], wbuf[:, o: o + 128], xn_sl(k),
                                start=(k == 0), stop=(k == 7))
                        return last
                    m = sch.add("tensor", mmg, waits=pw(b, [ld, xn]), inc=sPE)
                    gpend.append((f, b, m))
                    while len(gpend) > 3:
                        f2, b2, m2 = gpend.pop(0)
                        gelu_last = sch.add(
                            "scalar", lambda e, f=f2, b=b2: e.activation(
                                ARENA[:, f * TC:(f + 1) * TC], PS[b][:, :],
                                AF.Gelu, bias=b1c[:, f:f + 1], scale=1.0),
                            waits=[m2] + zspill_w, inc=sACT)
                        zspill_w = []
                        ps_free[b2] = gelu_last
                wbuf_use(lo, lo + 2048, m)
                if fp % 2 == 1:
                    yield
            for f2, b2, m2 in gpend:
                gelu_last = sch.add("scalar", lambda e, f=f2, b=b2: e.activation(
                    ARENA[:, f * TC:(f + 1) * TC], PS[b][:, :], AF.Gelu,
                    bias=b1c[:, f:f + 1], scale=1.0), waits=[m2], inc=sACT)
                ps_free[b2] = gelu_last
            gpend = []
            yield
            pend = None

            def do_tail_c(d, a, lds2):
                par = d % 2
                spsl = sp[:, par * TC: par * TC + TC]
                res = sch.add("vector",
                              lambda e, par=par, spsl=spsl: e.scalar_tensor_tensor(
                                  xsl[:, par * TC:(par + 1) * TC], spsl, 0.5,
                                  xsl[:, par * TC:(par + 1) * TC],
                                  ALU.mult, ALU.add),
                              waits=[a, lds2], inc=sDVE)
                sp_war[par] = res
                xsl_war[par] = dma(
                    bass.AP(out_h.tensor, c * TC + d * 128 * L,
                            [[L, 128], [1, TC]]),
                    xsl[:, par * TC:(par + 1) * TC], semO, waits=[res])

            mm_q = []

            def do_evac_c(d, b, m):
                par = d % 2
                w = []
                if xsl_war[par] is not None:
                    w.append(xsl_war[par])
                lds2 = dma(xsl[:, par * TC:(par + 1) * TC],
                           bass.AP(x2_d.tensor, d * TC,
                                   [[ND * TC, 128], [1, TC]]),
                           semSL[par], waits=w)
                xsl_war[par] = None
                spsl = sp[:, par * TC: par * TC + TC]
                we = [m] + ([sp_war[par]] if sp_war[par] is not None else [])
                a = sch.add("scalar", lambda e, d=d, b=b, spsl=spsl: e.activation(
                    spsl, PS[b][:, :], AF.Identity, bias=b2c[:, d:d + 1],
                    scale=1.0), waits=we, inc=sACT)
                ps_free[b] = a
                return (d, a, lds2)

            for d in range(ND):
                ld0, lo0 = wload(
                    bass.AP(w2s.tensor, (d * 32) * 16384,
                            [[128, 128], [16384, 16], [1, 128]]),
                    (2 * d) % 4, 2048, None)
                ld1, lo1 = wload(
                    bass.AP(w2s.tensor, (d * 32 + 16) * 16384,
                            [[128, 128], [16384, 16], [1, 128]]),
                    (2 * d + 1) % 4, 2048, None)
                b = 6 + (d % 2)

                if len(mm_q) > 1:
                    if pend is not None:
                        do_tail_c(*pend)
                    pend = do_evac_c(*mm_q.pop(0))

                def mmg2(e, lo0=lo0, lo1=lo1, b=b):
                    last = None
                    for k in range(32):
                        lo = lo0 if k < 16 else lo1
                        o = lo + (k % 16) * 128
                        last = nc.tensor.matmul(
                            PS[b][:, :], wbuf[:, o: o + 128],
                            ARENA[:, k * TC:(k + 1) * TC],
                            start=(k == 0), stop=(k == 31))
                    return last
                m = sch.add("tensor", mmg2, waits=pw(b, [ld0, ld1, gelu_last]),
                            inc=sPE)
                wbuf_use(lo0, lo0 + 2048, m)
                wbuf_use(lo1, lo1 + 2048, m)
                mm_q.append((d, b, m))
                yield
            while mm_q:
                if pend is not None:
                    do_tail_c(*pend)
                pend = do_evac_c(*mm_q.pop(0))
            do_tail_c(*pend)

        def emit_m1(c, x_ready):
            rs, mr = emit_ln_rows(True, stats_dep=None)
            xn = emit_ln_apply_X(True, rs, mr)
            pair_order = list(range(8))
            for jp, blk in enumerate(pair_order):
                slot = jp % 4
                ld, lo = wload(wblk(wins, blk, 16), slot, 2048,
                               wins_ready if jp < 4 else None)
                m = None
                for h in range(2):
                    f = 2 * blk + h
                    b = (2 * jp + h) % 4
                    def mmg(e, lo=lo, h=h, b=b):
                        last = None
                        for k in range(8):
                            o = lo + (h * 8 + k) * 128
                            last = nc.tensor.matmul(
                                PS[b][:, :], wbuf[:, o: o + 128], xn_sl(k),
                                start=(k == 0), stop=(k == 7))
                        return last
                    m = sch.add("tensor", mmg, waits=pw(b, [ld, xn]), inc=sPE)
                    if f >= 16:
                        ei = f - 16
                        a = sch.add("scalar", lambda e, ei=ei, b=b: e.copy(
                            ARENA[:, ei * TC:(ei + 1) * TC], PS[b][:, :]),
                            waits=[m], inc=sACT)
                    else:
                        ei = f
                        a = sch.add("scalar", lambda e, ei=ei, b=b: e.copy(
                            XSYG[:, ei * (TC + 3) + 3: (ei + 1) * (TC + 3)],
                            PS[b][:, :]), waits=[m], inc=sACT)
                    ps_free[b] = a
                wbuf_use(lo, lo + 2048, m)
            sp_z = (semZS, 16 * (c + 1))
            HW = ND * TC // 2
            dma(bass.AP(xn_d.tensor, 0, [[ND * TC, 128], [1, HW]]),
                SCR[:, ND * TC: ND * TC + HW], semSP,
                waits=[xn, fence(semNL[0]), fence(semNL[1])])
            sp_xn = dma(bass.AP(xn_d.tensor, HW, [[ND * TC, 128], [1, HW]]),
                        SCR[:, ND * TC + HW: 2 * ND * TC], semSP)
            dma(bass.AP(x1_d.tensor, 0, [[ND * TC, 128], [1, HW]]),
                X[:, 0:HW], semSP,
                waits=[(sDVE, sDVE.v), fence(semSL[0]), fence(semSL[1])])
            sp_x1 = dma(bass.AP(x1_d.tensor, HW, [[ND * TC, 128], [1, HW]]),
                        X[:, HW:ND * TC], semSP)
            hr = None
            for ei in range(NE):
                hr = sch.add("scalar", lambda e, ei=ei: e.copy(
                    XSYG[:, ei * (TC + 3): ei * (TC + 3) + 3],
                    halos[:, ei * 3:(ei + 1) * 3]), inc=sACT)
            xs_ready = hr
            u_last = None
            for ei in range(NE):
                cp = ei % 2
                wcd = [cds_ready] if ei < 2 else []
                if cdsl_war[cp] is not None:
                    wcd.append(cdsl_war[cp])
                cld = dma(cdsl[:, cp * 512:(cp + 1) * 512],
                          bass.AP(cds.tensor, ei * 4 * 16384,
                                  [[128, 128], [16384, 4], [1, 128]]),
                          semC[cp], waits=wcd)
                b = 2 + ei % 2
                def convg(e, ei=ei, b=b, cp=cp):
                    last = None
                    for tap in range(4):
                        last = nc.tensor.matmul(
                            PS[b][:, :],
                            cdsl[:, cp * 512 + tap * 128: cp * 512 + (tap + 1) * 128],
                            XSYG[:, ei * (TC + 3) + tap: ei * (TC + 3) + tap + TC],
                            start=(tap == 0), stop=(tap == 3))
                    return last
                m = sch.add("tensor", convg, waits=pw(b, [xs_ready, cld]), inc=sPE)
                cdsl_war[cp] = m
                a = sch.add("scalar", lambda e, ei=ei, b=b: e.activation(
                    u_sl(ei), PS[b][:, :], AF.Silu, bias=cbc[:, ei:ei + 1],
                    scale=1.0), waits=[m], inc=sACT)
                ps_free[b] = a
                u_last = sch.add("scalar", lambda e, ei=ei: e.copy(
                    halos[:, ei * 3:(ei + 1) * 3],
                    XSYG[:, ei * (TC + 3) + TC: ei * (TC + 3) + TC + 3]),
                    inc=sACT)
            u_ready = u_last

            def xprojg(e):
                last = None
                for k in range(NE):
                    last = nc.tensor.matmul(
                        PS[5][0:64, :], wxp[:, k * 96: k * 96 + 64], u_sl(k),
                        start=(k == 0), stop=(k == NE - 1))
                for k in range(NE):
                    last = nc.tensor.matmul(
                        PS[4][0:32, :], wxp[:, k * 96 + 64: k * 96 + 96], u_sl(k),
                        start=(k == 0), stop=(k == NE - 1))
                return last
            m = sch.add("tensor", xprojg, waits=pw(5, pw(4, [u_ready])), inc=sPE)
            a1 = sch.add("scalar", lambda e: e.copy(dtrbf[:, :], PS[5][0:64, :]),
                         waits=[m], inc=sACT)
            ps_free[5] = a1
            wbc = [m]
            if bcw_prev[0] is not None:
                wbc.append(bcw_prev[0])
            a2 = sch.add("scalar", lambda e: e.copy(bcbf[:, :], PS[4][0:32, :]),
                         waits=wbc, inc=sACT)
            ps_free[4] = a2
            bcw = dma(bass.AP(bc_dram.tensor, 0, [[TC, 2], [2 * TC, 16], [1, TC]]),
                      bcbf[:, :], semSP, waits=[a2])
            bcw_prev[0] = bcw
            dt_dones, dtu_dones = {}, {}
            for ei in range(NE):
                b = 2 + ei % 2
                m = sch.add("tensor", lambda e, ei=ei, b=b: e.matmul(
                    PS[b][:, :], wdt[:, ei * 128:(ei + 1) * 128], dtrbf[:, :],
                    start=True, stop=True), waits=pw(b, [a1]), inc=sPE)
                par = ei % 2
                spsl = sp[:, par * TC: par * TC + TC]
                we = [m] + ([sp_war[par]] if sp_war[par] is not None else [])
                a = sch.add("scalar", lambda e, spsl=spsl, ei=ei, b=b: e.activation(
                    spsl, PS[b][:, :], AF.Exp, bias=bdtc[:, ei:ei + 1],
                    scale=1.0), waits=we, inc=sACT)
                ps_free[b] = a
                a = sch.add("scalar", lambda e, spsl=spsl, ei=ei: e.activation(
                    dtbf[:, ei * TC:(ei + 1) * TC], spsl, AF.Ln,
                    bias=consts[:, 1:2], scale=1.0), inc=sACT)
                sp_war[par] = a
                dt_dones[ei] = a
                dtu_dones[ei] = sch.add("vector", lambda e, ei=ei: e.tensor_mul(
                    dtu[:, ei * TC:(ei + 1) * TC], dtbf[:, ei * TC:(ei + 1) * TC],
                    u_sl(ei)), waits=[a], inc=sDVE)
            sp_u1 = dma(bass.AP(u_d.tensor, 0, [[NE * TC, 128], [1, 8 * TC]]),
                        ARENA[:, 8192:12288], semSP,
                        waits=[(sPE, sPE.v), (sDVE, sDVE.v)])
            sp_u = dma(bass.AP(u_d.tensor, 8 * TC, [[NE * TC, 128], [1, 8 * TC]]),
                       ARENA[:, 12288:16384], semSP)
            return dict(dt=dt_dones, dtu=dtu_dones, bcw=bcw, u_ready=u_ready,
                        sp_x1=sp_x1, sp_xn=sp_xn, sp_z=sp_z, sp_u=sp_u,
                        sp_u1=sp_u1, xn_f=xn)

        def emit_scan(c, m1, pumps):
            LA = 2
            bc_free = {}
            bc_ready = {}
            gate_done = {}
            dA_done = {}
            dA_war = {}
            h_war = {}
            tmp_war = {0: None, 1: None}
            last_idacc = {}
            dt_dones = m1["dt"]
            dtu_dones = m1["dtu"]
            bcw = m1["bcw"]
            u_ready = m1["u_ready"]

            def bcast_pair(np5, half, cnt):
                return bass.AP(BC, (2 * np5 + half) * TC,
                               [[10 * TC, 128], [0, cnt], [1, TC]])

            def pump():
                for g in pumps:
                    try:
                        next(g)
                        return
                    except StopIteration:
                        continue

            tg0 = 0
            zload_done = {}
            for gi, g in enumerate(GROUPS):
                ng = len(g)
                for bi, e_ in enumerate(g):
                    bank = bi
                    cp = e_ % 2
                    wu = [m1["sp_u1"] if e_ < 8 else m1["sp_u"]]
                    if cdsl_war[cp] is not None:
                        wu.append(cdsl_war[cp])
                    ul = dma(cdsl[:, cp * 512:(cp + 1) * 512],
                             bass.AP(u_d.tensor, e_ * TC,
                                     [[NE * TC, 128], [1, TC]]),
                             semC[cp], waits=wu)
                    mu_ = sch.add("tensor", lambda e, e_=e_, bank=bank, cp=cp: e.matmul(
                        PS[bank][:, :], ddiag[:, e_ * 128:(e_ + 1) * 128],
                        cdsl[:, cp * 512:(cp + 1) * 512],
                        start=True, stop=False, skip_group_check=True),
                        waits=pw(bank, [ul]), inc=sPE)
                    cdsl_war[cp] = mu_
                zls = {}

                def emit_zls(g=g):
                    for bi, e_ in enumerate(g[:2]):
                        par = e_ % 2
                        wz = [m1["sp_z"]]
                        if cdsl_war[par] is not None:
                            wz.append(cdsl_war[par])
                        zls[e_] = dma(cdsl[:, par * 512:(par + 1) * 512],
                                      bass.AP(z_d.tensor, e_ * TC,
                                              [[NE * TC, 128], [1, TC]]),
                                      semZL[par], waits=wz)
                        cdsl_war[par] = None
                if gi > 0:
                    emit_zls()
                seq = [(n, i) for n in range(NST) for i in range(ng)]
                for si in range(len(seq) + LA):
                    if gi == 0 and si == 12 * ng:
                        emit_zls()
                    if si < len(seq):
                        n, i = seq[si]
                        if i == 0:
                            np5 = n % 5
                            wb = [bcw]
                            if np5 in bc_free:
                                wb.append(bc_free[np5])
                            bc_ready[n] = dma(
                                BC[:, 2 * np5 * TC: (2 * np5 + 2) * TC],
                                b_bcpair(n), semB[np5], waits=wb)
                        tg = tg0 + si
                        tp = tg % 3
                        wa = [dt_dones[g[i]]]
                        if tp in dA_war:
                            wa.append(dA_war[tp])
                        dA_done[tg] = sch.add(
                            "scalar", lambda e, e_=g[i], n=n, tp=tp: e.activation(
                                dA2[:, tp * TC:(tp + 1) * TC],
                                dtbf[:, e_ * TC:(e_ + 1) * TC], AF.Exp,
                                bias=0.0,
                                scale=aneg[:, e_ * NST + n: e_ * NST + n + 1]),
                            waits=wa, inc=sACT)
                    if si >= LA:
                        n, i = seq[si - LA]
                        tg = tg0 + si - LA
                        tp = tg % 3
                        e_ = g[i]
                        idx = e_ * NST + n
                        np5 = n % 5
                        if i == 0:
                            gb = g[0] * TC
                            sch.add("vector",
                                    lambda e, gb=gb, np5=np5, ng=ng: e.tensor_tensor(
                                        dBu[:, 0:ng * TC], dtu[:, gb: gb + ng * TC],
                                        bcast_pair(np5, 0, ng), ALU.mult),
                                    waits=[bc_ready[n], dtu_dones[g[ng - 1]]],
                                    inc=sDVE)
                        ws = [dA_done[tg]]
                        if i in h_war:
                            ws.append(h_war[i])
                        sc = sch.add(
                            "vector", lambda e, tp=tp, i=i, idx=idx: e.tensor_tensor_scan(
                                hbuf[:, i * TC:(i + 1) * TC],
                                dA2[:, tp * TC:(tp + 1) * TC],
                                dBu[:, i * TC:(i + 1) * TC],
                                hstate[:, idx:idx + 1],
                                ALU.mult, ALU.add), waits=ws, inc=sDVE)
                        dA_war[tp] = sc
                        cy = sch.add("scalar", lambda e, i=i, idx=idx: e.copy(
                            hstate[:, idx:idx + 1],
                            hbuf[:, i * TC + TC - 1: i * TC + TC]),
                            waits=[sc], inc=sACT)
                        h_war[i] = cy
                        if i == ng - 1:
                            rp = n % 2
                            ybuf = tmpb if rp == 0 else XSYG
                            yb = 0 if rp == 0 else 12 * TC
                            wy = []
                            if tmp_war[rp] is not None:
                                wy.append(tmp_war[rp])
                            ym = sch.add(
                                "vector",
                                lambda e, np5=np5, ng=ng, ybuf=ybuf, yb=yb:
                                e.tensor_tensor(
                                    ybuf[:, yb:yb + ng * TC], hbuf[:, 0:ng * TC],
                                    bcast_pair(np5, 1, ng), ALU.mult),
                                waits=wy, inc=sDVE)
                            bc_free[np5] = ym
                            mmi = None
                            for bi2, e2 in enumerate(g):
                                mmi = sch.add(
                                    "tensor",
                                    lambda e, bi2=bi2, n=n, ybuf=ybuf, yb=yb:
                                    e.matmul(
                                        PS[bi2][:, :], eye[:, :],
                                        ybuf[:, yb + bi2 * TC:yb + (bi2 + 1) * TC],
                                        start=False, stop=(n == NST - 1),
                                        skip_group_check=True),
                                    waits=[ym] if bi2 == 0 else [], inc=sPE)
                                last_idacc[e2] = mmi
                            tmp_war[rp] = mmi
                            pump()
                tg0 += len(seq)
                for bi, e_ in enumerate(g):
                    par = e_ % 2
                    if e_ in zls:
                        zl = zls[e_]
                    else:
                        wz = [m1["sp_z"]]
                        if cdsl_war[par] is not None:
                            wz.append(cdsl_war[par])
                        zl = dma(cdsl[:, par * 512:(par + 1) * 512],
                                 bass.AP(z_d.tensor, e_ * TC,
                                         [[NE * TC, 128], [1, TC]]),
                                 semZL[par], waits=wz)
                    wsil = [zl]
                    if (e_ - 2) in gate_done:
                        wsil.append(gate_done[e_ - 2])
                    az = sch.add("scalar", lambda e, par=par: e.activation(
                        zq[:, par * TC:(par + 1) * TC],
                        cdsl[:, par * 512:(par + 1) * 512], AF.Silu,
                        bias=0.0, scale=1.0), waits=wsil, inc=sACT)
                    gv = sch.add(
                        "vector", lambda e, e_=e_, bi=bi, par=par: e.tensor_mul(
                            XSYG[:, e_ * TC:(e_ + 1) * TC], PS[bi][:, :],
                            zq[:, par * TC:(par + 1) * TC]),
                        waits=[az, last_idacc[e_]], inc=sDVE)
                    gate_done[e_] = gv
                    cdsl_war[par] = gv
                    ps_free[bi] = gv
            return (sDVE, sDVE.v)

        def emit_m2(c, m1, gates, pref=None):
            m2stats = []
            for pos, d in enumerate([0, 1, 4, 5, 2, 3, 6, 7]):
                slot = d % 4
                if pref is not None and d in pref:
                    ld, lo = pref[d]
                else:
                    ld, lo = wload(wblk(wouts, d, 16), slot, 2048,
                                   wouts_ready if d < 4 else None)
                lane = d % 4
                if lane < 2:
                    xnv, xnw = xnsl[:, lane * TC:(lane + 1) * TC], xnsl_war
                    x1v, x1w = xsl[:, lane * TC:(lane + 1) * TC], xsl_war
                    lk = lane
                else:
                    lk = lane - 2
                    xnv, xnw = cdsl[:, lk * 512:(lk + 1) * 512], cdsl_war
                    x1v, x1w = sp[:, lk * TC:(lk + 1) * TC], sp_war
                if pref is not None and ("ldn", d) in pref:
                    ldn = pref[("ldn", d)]
                else:
                    ldw = [m1["sp_xn"]]
                    if lane >= 2 and d < 4:
                        ldw.append(gates)
                    if xnw[lk] is not None:
                        ldw.append(xnw[lk])
                    ldn = dma(xnv, bass.AP(xn_d.tensor, d * TC,
                                           [[ND * TC, 128], [1, TC]]),
                              semNL[lane % 2], waits=ldw)
                if pref is not None and ("lds", d) in pref:
                    lds = pref[("lds", d)]
                else:
                    ldw = [m1["sp_x1"]]
                    if x1w[lk] is not None:
                        ldw.append(x1w[lk])
                    lds = dma(x1v, bass.AP(x1_d.tensor, d * TC,
                                           [[ND * TC, 128], [1, TC]]),
                              semSL[lane % 2], waits=ldw)
                b = d % 2
                def mmg3(e, lo=lo, b=b, xnv=xnv):
                    for k in range(NE):
                        nc.tensor.matmul(
                            PS[b][:, :], wbuf[:, lo + k * 128: lo + (k + 1) * 128],
                            XSYG[:, k * TC:(k + 1) * TC],
                            start=(k == 0), stop=False)
                    return nc.tensor.matmul(
                        PS[b][:, :], eye[:, :], xnv,
                        start=False, stop=True)
                m = sch.add("tensor", mmg3, waits=pw(b, [ld, ldn, gates]), inc=sPE)
                wbuf_use(lo, lo + 2048, m)
                xnw[lk] = m
                res = sch.add("vector", lambda e, x1v=x1v, b=b: e.tensor_add(
                    x1v, x1v, PS[b][:, :]),
                    waits=[m, lds], inc=sDVE)
                ps_free[b] = res
                cast = sch.add("vector", lambda e, x1v=x1v, pos=pos:
                               e.tensor_copy(SCR[:, pos * TC:(pos + 1) * TC],
                                             x1v), waits=[res], inc=sDVE)
                sq = sch.add("scalar", lambda e, x1v=x1v, pos=pos: e.activation(
                    SCR[:, (8 + pos) * TC:(9 + pos) * TC], x1v, AF.Square,
                    bias=0.0, scale=1.0), waits=[res], inc=sACT)
                m2stats.append((pos, cast, sq))
                st = dma(bass.AP(x2_d.tensor, d * TC,
                                 [[ND * TC, 128], [1, TC]]),
                         x1v, semSP, waits=[res, cast, sq])
                x1w[lk] = st
            w6 = pw(6, [])
            w7 = pw(7, [])
            smm = None
            for pos, cast, sq in m2stats:
                sch.add("tensor", lambda e, pos=pos: e.matmul(
                    PS[6][0:1, :], onescolb[:, 0:1],
                    SCR[:, pos * TC:(pos + 1) * TC],
                    start=(pos == 0), stop=(pos == ND - 1)),
                    waits=(w6 + [cast]) if pos == 0 else [cast], inc=sPE)
                smm = sch.add("tensor", lambda e, pos=pos: e.matmul(
                    PS[7][0:1, :], onescolb[:, 0:1],
                    SCR[:, (8 + pos) * TC:(9 + pos) * TC],
                    start=(pos == 0), stop=(pos == ND - 1)),
                    waits=(w7 + [sq]) if pos == 0 else [sq], inc=sPE)
            if pref is not None:
                pref["done"] = True
            clds = {}
            for d in range(2):
                par = d % 2
                w = []
                if xsl_war[par] is not None:
                    w.append(xsl_war[par])
                clds[d] = dma(xsl[:, par * TC:(par + 1) * TC],
                              bass.AP(x2_d.tensor, d * TC,
                                      [[ND * TC, 128], [1, TC]]),
                              semSL[par], waits=w)
                xsl_war[par] = None
            return emit_stats_evac(smm), clds

        prev_gates = None
        m1_state = None
        wp_prev = None
        clds_prev = None
        stats_x2_dep = None
        genA_next = None
        xin_prev = None
        for c in range(NCH):
            if c == 0:
                w = [setup_a, setup_v]
                xin = dma(X[:, :], _xio(x_h, 0), semX, waits=w)
                for d in range(ND):
                    emit_stats_d(d, xin)
                gA = gen_ffn_A(0, xin)
                for _ in gA:
                    pass
                r1 = gen_ffn_A.res
            else:
                stats_x2_dep, clds_prev = emit_m2(c - 1, m1_state, prev_gates,
                                                  pref=wp_prev)
                wp_prev = None
                for _ in genA_next:
                    pass
                r1 = gen_ffn_A.res
            m1_state = emit_m1(c, r1)
            zh_gen = gen_zh(c, prev_gates, m1_state["xn_f"])
            pumps = [zh_gen]
            genC_cur = None
            if c > 0:
                genC_cur = gen_ffn_C(c - 1, stats_x2_dep, pre_lds=clds_prev,
                                     zh=zh_state)
                pumps.append(genC_cur)
            if c + 1 < NCH:
                wp_cur = {}

                def genA_wrap(c=c, wp=wp_cur, ms=m1_state):
                    xin = dma(X[:, :], _xio(x_h, c + 1), semX,
                              waits=[ms["sp_x1"], (sACT, sACT.v)])
                    for d in range(ND):
                        emit_stats_d(d, xin)
                    yield
                    yield from gen_ffn_A(c + 1, xin, wout_pref=wp, mstate=ms)
                genA_next = genA_wrap()
                wp_prev = wp_cur
                import os as _os
                if _os.environ.get("NO_APUMP") != "1":
                    pumps.append(genA_next)
            prev_gates = emit_scan(c, m1_state, pumps)
            for _ in zh_gen:
                pass
            if genC_cur is not None:
                for _ in genC_cur:
                    pass
        stats_x2_dep, clds_prev = emit_m2(NCH - 1, m1_state, prev_gates,
                                          pref=wp_prev)
        for _ in gen_ffn_C(NCH - 1, stats_x2_dep, pre_lds=clds_prev,
                           zh=zh_state):
            pass
        sch.add("sync", lambda e: e.nop(), waits=[fence(semO)])

        with nc.Block() as block:
            sch.emit(block)

    return nc


def _prep_shared(conv_w, conv_b, w_xproj, w_dt, b_dt, a_log, d_skip, w_out,
                 ln_w, ln_b, W1, b1, W2, b2, w_in):
    bf = ml_dtypes.bfloat16
    d = {}
    d["w1blk"] = np.ascontiguousarray(
        W1.T.reshape(8, 128, NF, 128).transpose(2, 0, 1, 3))
    d["w2blk"] = np.ascontiguousarray(
        W2.T.reshape(32, 128, ND, 128).transpose(2, 0, 1, 3))
    d["winblk"] = np.ascontiguousarray(
        w_in.T.reshape(8, 128, NF, 128).transpose(2, 0, 1, 3))
    d["woutblk"] = np.ascontiguousarray(
        w_out.T.reshape(16, 128, ND, 128).transpose(2, 0, 1, 3))
    d["wxpblk"] = np.ascontiguousarray(w_xproj.T.reshape(16, 128, 96))
    d["wdtblk"] = np.ascontiguousarray(
        w_dt.T.reshape(64, 16, 128).transpose(1, 0, 2))
    idx = np.arange(128)
    cd = np.zeros((NE, 4, 128, 128), np.float32)
    for e_ in range(NE):
        for tap in range(4):
            cd[e_, tap, idx, idx] = conv_w[e_ * 128:(e_ + 1) * 128, tap]
    d["cdiag"] = cd
    dd = np.zeros((128, NE * 128), np.float32)
    for e_ in range(NE):
        dd[idx, e_ * 128 + idx] = d_skip[e_ * 128:(e_ + 1) * 128]
    d["ddiag"] = dd
    d["alogc"] = np.ascontiguousarray(
        a_log.reshape(NE, 128, NST).transpose(1, 0, 2).reshape(128, NE * NST))
    d["b1c"] = np.ascontiguousarray(b1.reshape(NF, 128).T)
    d["b2c"] = np.ascontiguousarray(b2.reshape(ND, 128).T)
    d["cbc"] = np.ascontiguousarray(conv_b.reshape(NE, 128).T)
    d["bdtc"] = np.ascontiguousarray(b_dt.reshape(NE, 128).T)
    d["lnwc"] = np.ascontiguousarray(ln_w.reshape(ND, 128).T)
    d["lnbc"] = np.ascontiguousarray(ln_b.reshape(ND, 128).T)
    d["eye128"] = np.eye(128, dtype=bf)
    d["onescol"] = np.ones((128, 1), np.float32)
    d["onescolb"] = np.ones((128, 1), dtype=bf)
    return d


def kernel(L0_attn_x, L0_attn_ln_w, L0_attn_ln_b, L0_attn_w_in, L0_attn_conv_w,
           L0_attn_conv_b, L0_attn_w_xproj, L0_attn_w_dt, L0_attn_b_dt,
           L0_attn_a_log, L0_attn_d_skip, L0_attn_w_out, L0_ffn_W1, L0_ffn_b1,
           L0_ffn_W2, L0_ffn_b2):
    x = np.asarray(L0_attn_x, np.float32)
    B, L, _ = x.shape
    nc = build_nc(L)
    shared = _prep_shared(
        np.asarray(L0_attn_conv_w, np.float32),
        np.asarray(L0_attn_conv_b, np.float32),
        np.asarray(L0_attn_w_xproj, np.float32),
        np.asarray(L0_attn_w_dt, np.float32),
        np.asarray(L0_attn_b_dt, np.float32),
        np.asarray(L0_attn_a_log, np.float32),
        np.asarray(L0_attn_d_skip, np.float32),
        np.asarray(L0_attn_w_out, np.float32),
        np.asarray(L0_attn_ln_w, np.float32),
        np.asarray(L0_attn_ln_b, np.float32),
        np.asarray(L0_ffn_W1, np.float32),
        np.asarray(L0_ffn_b1, np.float32),
        np.asarray(L0_ffn_W2, np.float32),
        np.asarray(L0_ffn_b2, np.float32),
        np.asarray(L0_attn_w_in, np.float32))
    in_maps = []
    for b in range(B):
        m = dict(shared)
        m["x"] = np.ascontiguousarray(x[b].T)
        in_maps.append(m)
    res = run_bass_kernel_spmd(nc, in_maps, core_ids=list(range(B)))
    out = np.stack([np.asarray(r["out"]).T for r in res.results], axis=0)
    return np.ascontiguousarray(out.astype(np.float32))


# version 32
# speedup vs baseline: 1.0181x; 1.0181x over previous
import contextlib
import numpy as np
import ml_dtypes
import concourse.bass as bass
import concourse.mybir as mybir
from concourse.bass_utils import run_bass_kernel_spmd

F32 = mybir.dt.float32
BF16 = mybir.dt.bfloat16
AF = mybir.ActivationFunctionType
ALU = mybir.AluOpType

D = 1024
FF = 4096
E = 2048
NST = 16
TC = 512
ND = D // 128
NF = FF // 128
NE = E // 128
EPS = 1e-5
INV_D = 1.0 / D
ENGINES = ("sync", "tensor", "scalar", "vector", "gpsimd")
GROUPS = [list(range(4 * i, 4 * i + 4)) for i in range(4)]


class Sem:
    __slots__ = ("h", "v", "name")


class Sched:
    def __init__(self, nc, st):
        self.nc = nc
        self.st = st
        self.items = {e: [] for e in ENGINES}

    def sem(self, name):
        s = Sem()
        s.h = self.st.enter_context(self.nc.semaphore(name))
        s.v = 0
        s.name = name
        return s

    def add(self, eng, fn, waits=(), inc=None, amt=1):
        w = [x for x in waits if x is not None]
        self.items[eng].append((tuple(w), fn, inc, amt))
        if inc is not None:
            inc.v += amt
            return (inc, inc.v)
        return None

    def emit(self, block):
        for eng in ENGINES:
            items = self.items[eng]
            if not items:
                continue
            section = getattr(block, eng)

            def make(items=items):
                def _(engobj):
                    last = {}
                    for waits, fn, inc, amt in items:
                        for (s, v) in waits:
                            if v > last.get(id(s), 0):
                                engobj.wait_ge(s.h, v)
                                last[id(s)] = v
                        inst = fn(engobj)
                        if inc is not None:
                            inst.then_inc(inc.h, amt)
                return _
            section(make())


def build_nc(L):
    NCH = L // TC
    nc = bass.Bass("TRN2", target_bir_lowering=False,
                   detect_race_conditions=False)

    di = lambda n, s, d: nc.dram_tensor(n, s, d, kind="ExternalInput").ap()
    x_h = di("x", [D, L], F32)
    w1_h = di("w1blk", [NF, 8, 128, 128], F32)
    w2_h = di("w2blk", [ND, 32, 128, 128], F32)
    win_h = di("winblk", [NF, 8, 128, 128], F32)
    wout_h = di("woutblk", [ND, 16, 128, 128], F32)
    wxp_h = di("wxpblk", [16, 128, 96], F32)
    wdt_h = di("wdtblk", [16, 64, 128], F32)
    cdiag_h = di("cdiag", [NE, 4, 128, 128], F32)
    ddiag_h = di("ddiag", [128, NE * 128], F32)
    alog_h = di("alogc", [128, NE * NST], F32)
    b1c_h = di("b1c", [128, NF], F32)
    b2c_h = di("b2c", [128, ND], F32)
    cbc_h = di("cbc", [128, NE], F32)
    bdtc_h = di("bdtc", [128, NE], F32)
    lnwc_h = di("lnwc", [128, ND], F32)
    lnbc_h = di("lnbc", [128, ND], F32)
    eye_h = di("eye128", [128, 128], BF16)
    onescol_h = di("onescol", [128, 1], F32)
    onescolb_h = di("onescolb", [128, 1], BF16)

    out_h = nc.dram_tensor("out", [D, L], F32, kind="ExternalOutput").ap()

    ds = lambda n, s, d: nc.dram_tensor(n, s, d, kind="Internal").ap()
    w1s = ds("w1s", [NF, 8, 128, 128], BF16)
    w2s = ds("w2s", [ND, 32, 128, 128], BF16)
    wins = ds("wins", [NF, 8, 128, 128], BF16)
    wouts = ds("wouts", [ND, 16, 128, 128], BF16)
    cds = ds("cds", [NE, 4, 128, 128], BF16)
    bc_dram = ds("bc_dram", [32, TC], BF16)
    x1_d = ds("x1_d", [128, ND * TC], F32)
    x2_d = ds("x2_d", [128, ND * TC], F32)
    xn_d = ds("xn_d", [128, ND * TC], BF16)
    z_d = ds("z_d", [128, NE * TC], BF16)
    u_d = ds("u_d", [128, NE * TC], BF16)

    def wblk(ap, i, kc):
        return bass.AP(ap.tensor, i * kc * 16384,
                       [[128, 128], [16384, kc], [1, 128]])

    def _xio(ap, c):
        return bass.AP(ap.tensor, c * TC, [[L, 128], [128 * L, ND], [1, TC]])

    def b_bcpair(n):
        return bass.AP(bc_dram.tensor, 2 * n * TC, [[0, 128], [TC, 2], [1, TC]])

    with contextlib.ExitStack() as st:
        sb = lambda n, s, d: st.enter_context(nc.sbuf_tensor(n, s, d))
        X = sb("X", [128, ND * TC], F32)
        dA2 = sb("dA2", [128, 3 * TC], F32)
        hstate = sb("hstate", [128, NE * NST], F32)
        aneg = sb("aneg", [128, NE * NST], F32)
        thrs = sb("thrs", [128, 4], F32)
        absacc = sb("absacc", [128, 64], F32)
        repRS = sb("repRS", [128, TC], F32)
        repMR = sb("repMR", [128, TC], BF16)
        sp = sb("sp", [128, 2 * TC], F32)
        rowf = sb("rowf", [1, 6 * TC], F32)
        xsl = sb("xsl", [128, 2 * TC], F32)
        zq = sb("zq", [128, 2 * TC], F32)
        xnsl = sb("xnsl", [128, 2 * TC], BF16)
        b1c = sb("b1c_s", [128, NF], F32)
        b2c = sb("b2c_s", [128, ND], F32)
        cbc = sb("cbc_s", [128, NE], F32)
        bdtc = sb("bdtc_s", [128, NE], F32)
        lnwc = sb("lnwc_s", [128, ND], F32)
        lnbc = sb("lnbc_s", [128, ND], F32)
        onescol = sb("onescol_s", [128, 1], F32)
        onescolb = sb("onescolb_s", [128, 1], BF16)
        ones1x128 = sb("ones1x128_s", [1, 128], F32)
        consts = sb("consts_s", [128, 2], F32)

        SCR = sb("SCR", [128, 2 * ND * TC], BF16)
        ARENA = sb("ARENA", [128, NF * TC], BF16)
        dtbf = sb("dtbf", [128, NE * TC], BF16)
        XSYG = sb("XSYG", [128, NE * (TC + 3)], BF16)
        dtu = sb("dtu", [128, NE * TC], BF16)
        wbuf = sb("wbuf", [128, 8192], BF16)
        BC = sb("BC", [128, 10 * TC], BF16)
        dBu = sb("dBu", [128, 4 * TC], BF16)
        hbuf = sb("hbuf", [128, 4 * TC], BF16)
        tmpb = sb("tmpb", [128, 4 * TC], BF16)
        xsq = sb("xsq", [128, 2 * TC], BF16)
        cdsl = sb("cdsl", [128, 2 * 512], BF16)
        ddiag = sb("ddiag_s", [128, NE * 128], BF16)
        wxp = sb("wxp_s", [128, 16 * 96], BF16)
        wdt = sb("wdt_s", [64, 16 * 128], BF16)
        eye = sb("eye_s", [128, 128], BF16)
        bcbf = sb("bcbf", [32, TC], BF16)
        dtrbf = sb("dtrbf", [64, TC], BF16)
        halos = sb("halos", [128, NE * 3], BF16)
        PS = [st.enter_context(nc.psum_tensor(f"ps{i}", [128, TC], F32))
              for i in range(8)]

        sch = Sched(nc, st)
        sPE = sch.sem("sPE")
        sACT = sch.sem("sACT")
        sDVE = sch.sem("sDVE")
        sGP = sch.sem("sGP")
        semX = sch.sem("semX")
        semO = sch.sem("semO")
        semW = [sch.sem(f"semW{i}") for i in range(4)]
        semB = [sch.sem(f"semB{i}") for i in range(5)]
        semZS = sch.sem("semZS")
        zh_state = {"mm": None}
        semC = [sch.sem(f"semC{i}") for i in range(2)]
        semSL = [sch.sem(f"semSL{i}") for i in range(2)]
        semNL = [sch.sem(f"semNL{i}") for i in range(2)]
        semZL = [sch.sem(f"semZL{i}") for i in range(2)]
        semSP = sch.sem("semSP")
        semSCR = sch.sem("semSCR")
        semS = [sch.sem(f"semS{i}") for i in range(4)]
        semMISC = sch.sem("semMISC")

        def fence(s):
            return (s, s.v)

        ps_free = {i: None for i in range(8)}
        wbuf_deps = []
        sp_war = {0: None, 1: None}
        bcw_prev = [None]
        xsl_war = {0: None, 1: None}
        cdsl_war = {0: None, 1: None}
        xnsl_war = {0: None, 1: None}
        zsl_war = {0: None, 1: None}
        sq_war = {0: None, 1: None}
        sq_idx = [0]

        def wbuf_claim(lo, hi):
            w, keep = [], []
            for ent in wbuf_deps:
                if ent[0] < hi and lo < ent[1]:
                    w.append(ent[2])
                else:
                    keep.append(ent)
            wbuf_deps[:] = keep
            return w

        def wbuf_use(lo, hi, dep):
            wbuf_deps.append((lo, hi, dep))

        def pw(bank, extra=()):
            w = list(extra)
            if ps_free[bank] is not None:
                w.append(ps_free[bank])
            return w

        def dma(dst, src, sem, waits=()):
            sch.add("sync", lambda e, d=dst, s=src: e.dma_start(d, s),
                    waits=waits, inc=sem, amt=16)
            return fence(sem)

        for dst, src in ((b1c, b1c_h), (b2c, b2c_h), (cbc, cbc_h), (bdtc, bdtc_h),
                         (lnwc, lnwc_h), (lnbc, lnbc_h), (onescol, onescol_h),
                         (onescolb, onescolb_h), (eye, eye_h)):
            dma(dst[:, :], src, semMISC)
        sch.add("vector", lambda e: e.memset(hstate[:, :], 0.0), inc=sDVE)
        sch.add("vector", lambda e: e.memset(halos[:, :], 0.0), inc=sDVE)
        sch.add("vector", lambda e: e.memset(consts[:, 0:1], EPS), inc=sDVE)
        sch.add("vector", lambda e: e.memset(consts[:, 1:2], 1.0), inc=sDVE)
        sch.add("vector", lambda e: e.memset(ones1x128[0:1, :], 1.0), inc=sDVE)

        def w_mean_pass(v_src, nblk, width, acc_off):
            acts = []
            ns = 4096 // width
            for i in range(nblk):
                par = i % ns
                w = [acts[i - ns]] if i >= ns else [(sDVE, sDVE.v)]
                ld = dma(X[:, par * width: par * width + width], v_src(i),
                         semS[par], waits=w)
                a = sch.add("scalar", lambda e, p=par, wd=width, i=i: e.activation(
                    SCR[:, p * wd: p * wd + wd], X[:, p * wd: p * wd + wd],
                    AF.Abs, bias=0.0, scale=1.0,
                    accum_out=absacc[:, acc_off + i: acc_off + i + 1]),
                    waits=[ld], inc=sACT)
                acts.append(a)
            return acts[-1]

        def scale_math(off, n, ti, nelem, prev):
            v = sch.add("vector", lambda e, off=off, n=n: e.tensor_reduce(
                absacc[:, 63:64], absacc[:, off:off + n],
                mybir.AxisListType.X, ALU.add), waits=[prev], inc=sDVE)
            b = 6
            m = sch.add("tensor", lambda e: e.matmul(
                PS[b][0:1, 0:1], onescol[:, 0:1], absacc[:, 63:64],
                start=True, stop=True),
                waits=pw(b, [v, fence(semMISC)]), inc=sPE)
            a = sch.add("scalar", lambda e, nelem=nelem: e.activation(
                rowf[0:1, 0:1], PS[b][0:1, 0:1], AF.Copy, bias=0.0,
                scale=1.0 / nelem), waits=[m], inc=sACT)
            ps_free[b] = a
            v = sch.add("vector", lambda e: e.tensor_scalar_max(
                rowf[0:1, 0:1], rowf[0:1, 0:1], 1e-8), waits=[a], inc=sDVE)
            a = sch.add("scalar", lambda e: e.mul(
                rowf[0:1, 0:1], rowf[0:1, 0:1], 0.5), waits=[v], inc=sACT)
            m = sch.add("tensor", lambda e: e.matmul(
                PS[b][:, 0:1], ones1x128[0:1, :], rowf[0:1, 0:1],
                start=True, stop=True), waits=pw(b, [a]), inc=sPE)
            a = sch.add("scalar", lambda e, ti=ti: e.copy(
                thrs[:, ti:ti + 1], PS[b][:, 0:1]), waits=[m], inc=sACT)
            ps_free[b] = a
            return sch.add("scalar", lambda e, ti=ti: e.mul(
                thrs[:, ti + 1:ti + 2], thrs[:, ti:ti + 1], -1.0), inc=sACT)

        def tern_pass(v_src, v_dst, nblk, width, ti, thr_ready):
            dves, outs = [], []
            ns = 4096 // width
            for i in range(nblk):
                par = i % ns
                w = [dves[i - ns]] if i >= ns else [thr_ready]
                ld = dma(X[:, par * width: par * width + width], v_src(i),
                         semS[par], waits=w)

                def tern(e, p=par, wd=width, ti=ti):
                    xs = X[:, p * wd: p * wd + wd]
                    nc.vector.tensor_scalar(SCR[:, p * wd: p * wd + wd], xs,
                                            thrs[:, ti:ti + 1], None, ALU.is_gt)
                    nc.vector.tensor_scalar(
                        SCR[:, 4096 + p * wd: 4096 + p * wd + wd], xs,
                        thrs[:, ti + 1:ti + 2], None, ALU.is_lt)
                    return nc.vector.tensor_sub(
                        ARENA[:, p * wd: p * wd + wd],
                        SCR[:, p * wd: p * wd + wd],
                        SCR[:, 4096 + p * wd: 4096 + p * wd + wd])
                w2 = [ld] + ([outs[i - ns]] if i >= ns else [])
                v = sch.add("vector", tern, waits=w2, inc=sDVE)
                dves.append(v)
                outs.append(dma(v_dst(i), ARENA[:, par * width: par * width + width],
                                semSCR, waits=[v]))
            return outs[-1]

        def cast_stream(v_src, v_dst, nblk, width):
            dves, outs = [], []
            ns = 4096 // width
            for i in range(nblk):
                par = i % ns
                w = [dves[i - ns]] if i >= ns else [(sDVE, sDVE.v)]
                ld = dma(X[:, par * width: par * width + width], v_src(i),
                         semS[par], waits=w)
                w2 = [ld] + ([outs[i - ns]] if i >= ns else [])
                v = sch.add("vector", lambda e, p=par, wd=width: e.tensor_copy(
                    ARENA[:, p * wd: p * wd + wd],
                    X[:, p * wd: p * wd + wd]), waits=w2, inc=sDVE)
                dves.append(v)
                outs.append(dma(v_dst(i), ARENA[:, par * width: par * width + width],
                                semSCR, waits=[v]))
            return outs[-1]

        a = w_mean_pass(lambda i: wblk(w1_h, i, 8), 32, 1024, 0)
        t1r = scale_math(0, 32, 0, FF * D, a)
        w1s_ready = tern_pass(lambda i: wblk(w1_h, i, 8),
                              lambda i: wblk(w1s, i, 8), 32, 1024, 0, t1r)
        a = w_mean_pass(lambda i: wblk(w2_h, i, 16), 16, 2048, 32)
        t2r = scale_math(32, 16, 2, D * FF, a)
        w2s_ready = tern_pass(lambda i: wblk(w2_h, i, 16),
                              lambda i: wblk(w2s, i, 16), 16, 2048, 2, t2r)
        wins_ready = cast_stream(lambda i: wblk(win_h, i, 8),
                                 lambda i: wblk(wins, i, 8), 32, 1024)
        wouts_ready = cast_stream(lambda i: wblk(wout_h, i, 16),
                                  lambda i: wblk(wouts, i, 16), 8, 2048)
        cds_ready = cast_stream(lambda i: wblk(cdiag_h, i, 8),
                                lambda i: wblk(cds, i, 8), 8, 1024)

        def resident_cast(src_ap, dst_ap, width, rows=128):
            ld = dma(X[0:rows, 0:width], src_ap, semS[0], waits=[(sDVE, sDVE.v)])
            return sch.add("vector", lambda e: e.tensor_copy(
                dst_ap, X[0:rows, 0:width]), waits=[ld], inc=sDVE)

        resident_cast(ddiag_h, ddiag[:, :], 2048)
        resident_cast(
            bass.AP(wxp_h.tensor, 0, [[96, 128], [128 * 96, 16], [1, 96]]),
            wxp[:, :], 16 * 96)
        resident_cast(
            bass.AP(wdt_h.tensor, 0, [[128, 64], [64 * 128, 16], [1, 128]]),
            wdt[:, :], 16 * 128, rows=64)
        ld = dma(X[:, 0:256], alog_h, semS[0], waits=[(sDVE, sDVE.v)])
        a = sch.add("scalar", lambda e: e.activation(
            aneg[:, :], X[:, 0:256], AF.Exp, bias=0.0, scale=1.0),
            waits=[ld], inc=sACT)
        sch.add("vector", lambda e: e.tensor_scalar_mul(
            aneg[:, :], aneg[:, :], -1.0), waits=[a], inc=sDVE)
        setup_v = (sDVE, sDVE.v)
        setup_a = (sACT, sACT.v)

        def r_(i):
            return rowf[0:1, i * TC:(i + 1) * TC]

        def xn_sl(k):
            return SCR[:, ND * TC + k * TC: ND * TC + (k + 1) * TC]

        def u_sl(k):
            return ARENA[:, 8192 + k * TC: 8192 + (k + 1) * TC]

        def emit_stats_d(d, src_ready, src=None, bx=4, bq=5, idx=None):
            idx = d if idx is None else idx
            src = X[:, d * TC:(d + 1) * TC] if src is None else src
            cast = sch.add("vector", lambda e, src=src, d=d: e.tensor_copy(
                SCR[:, d * TC:(d + 1) * TC], src),
                waits=[src_ready] if src_ready is not None else [], inc=sDVE)
            w = pw(bx, [cast, fence(semMISC)]) if idx == 0 else [cast]
            sch.add("tensor", lambda e, d=d, bx=bx, idx=idx: e.matmul(
                PS[bx][0:1, :], onescolb[:, 0:1], SCR[:, d * TC:(d + 1) * TC],
                start=(idx == 0), stop=(idx == ND - 1)), waits=w, inc=sPE)
            par = sq_idx[0] % 2
            sq_idx[0] += 1
            xslc = xsq[:, par * TC:(par + 1) * TC]
            wsq = [src_ready] if src_ready is not None else []
            if sq_war[par] is not None:
                wsq = wsq + [sq_war[par]]
            sq = sch.add("scalar", lambda e, src=src, xslc=xslc: e.activation(
                xslc, src, AF.Square, bias=0.0, scale=1.0),
                waits=wsq, inc=sACT)
            w2 = pw(bq, [sq]) if idx == 0 else [sq]
            mm = sch.add("tensor", lambda e, bq=bq, xslc=xslc, idx=idx: e.matmul(
                PS[bq][0:1, :], onescolb[:, 0:1], xslc,
                start=(idx == 0), stop=(idx == ND - 1)), waits=w2, inc=sPE)
            sq_war[par] = mm
            return mm

        def emit_stats_evac(mm):
            a = sch.add("scalar", lambda e: e.copy(r_(4), PS[6][0:1, :]),
                        waits=[mm], inc=sACT)
            a = sch.add("scalar", lambda e: e.copy(r_(5), PS[7][0:1, :]),
                        inc=sACT)
            ps_free[6] = a
            ps_free[7] = a
            return a

        def emit_ln_rows(mamba, from_rows=None, stats_dep=None):
            if from_rows is None:
                sch.add("scalar", lambda e: e.activation(
                    r_(0), PS[4][0:1, :], AF.Copy, bias=0.0, scale=INV_D),
                    waits=[(sPE, sPE.v)] if stats_dep is None else [stats_dep],
                    inc=sACT)
                a = sch.add("scalar", lambda e: e.activation(
                    r_(1), PS[5][0:1, :], AF.Copy, bias=0.0, scale=INV_D),
                    inc=sACT)
                ps_free[4] = a
                ps_free[5] = a
            else:
                sch.add("scalar", lambda e: e.activation(
                    r_(0), r_(4), AF.Copy, bias=0.0, scale=INV_D),
                    waits=[stats_dep], inc=sACT)
                a = sch.add("scalar", lambda e: e.activation(
                    r_(1), r_(5), AF.Copy, bias=0.0, scale=INV_D), inc=sACT)
            sch.add("scalar", lambda e: e.square(r_(2), r_(0)), inc=sACT)
            v = sch.add("vector", lambda e: e.tensor_sub(r_(1), r_(1), r_(2)),
                        waits=[(sACT, sACT.v)], inc=sDVE)
            a = sch.add("scalar", lambda e: e.activation(
                r_(2), r_(1), AF.Ln, bias=consts[0:1, 0:1], scale=1.0),
                waits=[v], inc=sACT)
            a = sch.add("scalar", lambda e: e.activation(
                r_(3), r_(2), AF.Exp, bias=0.0, scale=-0.5), inc=sACT)
            if mamba:
                v = sch.add("vector", lambda e: e.tensor_scalar_add(
                    r_(2), r_(1), EPS), waits=[a], inc=sDVE)
                v = sch.add("vector", lambda e: e.reciprocal(r_(2), r_(2)),
                            inc=sDVE)
                v = sch.add("vector", lambda e: e.tensor_mul(r_(2), r_(1), r_(2)),
                            inc=sDVE)
                a = sch.add("scalar", lambda e: e.activation(
                    r_(2), r_(2), AF.Ln, bias=consts[0:1, 0:1], scale=1.0),
                    waits=[v], inc=sACT)
                a = sch.add("scalar", lambda e: e.activation(
                    r_(2), r_(2), AF.Exp, bias=0.0, scale=-0.5), inc=sACT)
                v = sch.add("vector", lambda e: e.tensor_mul(r_(3), r_(3), r_(2)),
                            waits=[a], inc=sDVE)
            v = sch.add("vector", lambda e: e.tensor_mul(r_(2), r_(0), r_(3)),
                        waits=[(sACT, sACT.v)], inc=sDVE)
            m1 = sch.add("tensor", lambda e: e.matmul(
                PS[4][:, :], ones1x128[0:1, :], r_(3), start=True, stop=True),
                waits=pw(4, [v]), inc=sPE)
            m2 = sch.add("tensor", lambda e: e.matmul(
                PS[5][:, :], ones1x128[0:1, :], r_(2), start=True, stop=True),
                waits=pw(5, [v]), inc=sPE)
            rs = sch.add("scalar", lambda e: e.copy(repRS[:, :], PS[4][:, :]),
                         waits=[m1, (sDVE, sDVE.v)], inc=sACT)
            ps_free[4] = rs
            mr = sch.add("scalar", lambda e: e.copy(repMR[:, :], PS[5][:, :]),
                         waits=[m2], inc=sACT)
            ps_free[5] = mr
            return rs, mr

        def emit_ln_apply_X(mamba, rs, mr, extra=None):
            last = None
            for d in range(ND):
                sl = slice(d * TC, (d + 1) * TC)
                sch.add("vector", lambda e, sl=sl: e.tensor_mul(
                    SCR[:, sl], X[:, sl], repRS[:, :]),
                    waits=([rs, extra] if d == 0 else []), inc=sDVE)
                last = sch.add("vector", lambda e, d=d, sl=sl: e.tensor_sub(
                    xn_sl(d), SCR[:, sl], repMR[:, :]),
                    waits=[mr] if d == 0 else [], inc=sDVE)
                if mamba:
                    last = sch.add("vector", lambda e, d=d: e.tensor_scalar(
                        xn_sl(d), xn_sl(d), lnwc[:, d:d + 1], lnbc[:, d:d + 1],
                        ALU.mult, ALU.add), inc=sDVE)
            return last

        def wload(view, slot, width, ready_first, sem_idx=None):
            lo = slot * 2048
            w = ([ready_first] if ready_first is not None else []) \
                + wbuf_claim(lo, lo + width)
            f = dma(wbuf[:, lo:lo + width], view, semW[slot if sem_idx is None
                                                       else sem_idx], waits=w)
            return f, lo

        def gen_zh(c, prev_gates_f, xn_f):
            zpend = []
            last_mm = None
            for jp, blk in enumerate(range(8, 16)):
                slot = jp % 4
                ld, lo = wload(wblk(wins, blk, 16), slot, 2048,
                               wins_ready if jp < 4 else None)
                m = None
                for h in range(2):
                    b = 4 + (2 * jp + h) % 4

                    def mmg(e, lo=lo, h=h, b=b):
                        last = None
                        for k in range(8):
                            o = lo + (h * 8 + k) * 128
                            last = nc.tensor.matmul(
                                PS[b][:, :], wbuf[:, o: o + 128], xn_sl(k),
                                start=(k == 0), stop=(k == 7))
                        return last
                    m = sch.add("tensor", mmg, waits=pw(b, [ld, xn_f]),
                                inc=sPE)
                    last_mm = m
                    zpend.append((2 * blk + h - 16, b, m))
                    while len(zpend) > 2:
                        ei2, b2, m2 = zpend.pop(0)
                        a = sch.add("scalar", lambda e, ei=ei2, b=b2: e.copy(
                            ARENA[:, ei * TC:(ei + 1) * TC], PS[b][:, :]),
                            waits=[m2], inc=sACT)
                        ps_free[b2] = a
                wbuf_use(lo, lo + 2048, m)
                zh_state["mm"] = last_mm
                yield
            for ei2, b2, m2 in zpend:
                a = sch.add("scalar", lambda e, ei=ei2, b=b2: e.copy(
                    ARENA[:, ei * TC:(ei + 1) * TC], PS[b][:, :]),
                    waits=[m2], inc=sACT)
                ps_free[b2] = a
            w = [(sACT, sACT.v)]
            if prev_gates_f is not None:
                w.append(prev_gates_f)
            dma(bass.AP(z_d.tensor, 0, [[NE * TC, 128], [1, NE * TC]]),
                ARENA[:, 0:NE * TC], semZS, waits=w)
            zh_state["mm"] = last_mm

        def gen_ffn_A(c, x_ready, wout_pref=None, mstate=None):
            rs, mr = emit_ln_rows(False, stats_dep=None)
            yield
            xn = emit_ln_apply_X(False, rs, mr, extra=fence(semSP))
            yield
            gelu_last = None
            gpend = []
            for fp in range(NF // 2):
                slot = fp % 4
                ld, lo = wload(wblk(w1s, fp, 16), slot, 2048,
                               w1s_ready if fp < 4 else None)
                m = None
                for h in range(2):
                    f = 2 * fp + h
                    b = 4 + (f % 4)

                    def mmg(e, lo=lo, h=h, b=b):
                        last = None
                        for k in range(8):
                            o = lo + (h * 8 + k) * 128
                            last = nc.tensor.matmul(
                                PS[b][:, :], wbuf[:, o: o + 128], xn_sl(k),
                                start=(k == 0), stop=(k == 7))
                        return last
                    m = sch.add("tensor", mmg,
                                waits=pw(b, [ld, xn]), inc=sPE)
                    gpend.append((f, b, m))
                    while len(gpend) > 3:
                        f2, b2, m2 = gpend.pop(0)
                        gelu_last = sch.add(
                            "scalar", lambda e, f=f2, b=b2: e.activation(
                                ARENA[:, f * TC:(f + 1) * TC], PS[b][:, :],
                                AF.Gelu, bias=b1c[:, f:f + 1], scale=1.0),
                            waits=[m2], inc=sACT)
                        ps_free[b2] = gelu_last
                wbuf_use(lo, lo + 2048, m)
                if fp % 2 == 1:
                    yield
            for f2, b2, m2 in gpend:
                gelu_last = sch.add("scalar", lambda e, f=f2, b=b2: e.activation(
                    ARENA[:, f * TC:(f + 1) * TC], PS[b][:, :], AF.Gelu,
                    bias=b1c[:, f:f + 1], scale=1.0), waits=[m2], inc=sACT)
                ps_free[b2] = gelu_last
            gpend = []
            yield
            pend = None

            spend = []

            def do_tail(d, a):
                par = d % 2
                spsl = sp[:, par * TC: par * TC + TC]
                res = sch.add("vector",
                              lambda e, spsl=spsl: e.scalar_tensor_tensor(
                                  X[:, d * TC:(d + 1) * TC], spsl, 0.5,
                                  X[:, d * TC:(d + 1) * TC],
                                  ALU.mult, ALU.add), waits=[a], inc=sDVE)
                sp_war[par] = res
                spend.append((d, res))
                if len(spend) > 1:
                    emit_stats_d(*spend.pop(0))
                gen_ffn_A.res = res
                if d == ND - 1:
                    while spend:
                        emit_stats_d(*spend.pop(0))

            mm_q = []

            def do_evac(d, b, m):
                par = d % 2
                spsl = sp[:, par * TC: par * TC + TC]
                we = [m] + ([sp_war[par]] if sp_war[par] is not None else [])
                a = sch.add("scalar", lambda e, d=d, b=b, spsl=spsl: e.activation(
                    spsl, PS[b][:, :], AF.Identity, bias=b2c[:, d:d + 1],
                    scale=1.0), waits=we, inc=sACT)
                ps_free[b] = a
                return (d, a)

            for d in range(ND):
                ld0, lo0 = wload(
                    bass.AP(w2s.tensor, (d * 32) * 16384,
                            [[128, 128], [16384, 16], [1, 128]]),
                    (2 * d) % 4, 2048, w2s_ready if d < 2 else None)
                ld1, lo1 = wload(
                    bass.AP(w2s.tensor, (d * 32 + 16) * 16384,
                            [[128, 128], [16384, 16], [1, 128]]),
                    (2 * d + 1) % 4, 2048, w2s_ready if d < 2 else None)
                b = 6 + (d % 2)

                if len(mm_q) > 1:
                    if pend is not None:
                        do_tail(*pend)
                    pend = do_evac(*mm_q.pop(0))

                def mmg2(e, lo0=lo0, lo1=lo1, b=b):
                    last = None
                    for k in range(32):
                        lo = lo0 if k < 16 else lo1
                        o = lo + (k % 16) * 128
                        last = nc.tensor.matmul(
                            PS[b][:, :], wbuf[:, o: o + 128],
                            ARENA[:, k * TC:(k + 1) * TC],
                            start=(k == 0), stop=(k == 31))
                    return last
                m = sch.add("tensor", mmg2, waits=pw(b, [ld0, ld1, gelu_last]),
                            inc=sPE)
                wbuf_use(lo0, lo0 + 2048, m)
                wbuf_use(lo1, lo1 + 2048, m)
                mm_q.append((d, b, m))
                yield
            while mm_q:
                if pend is not None:
                    do_tail(*pend)
                pend = do_evac(*mm_q.pop(0))
            do_tail(*pend)
            if wout_pref is not None and "done" not in wout_pref:
                for d2 in range(2):
                    wout_pref[d2] = wload(wblk(wouts, d2, 16), d2 % 2, 2048,
                                          wouts_ready)
                if mstate is not None:
                    for d2 in range(2):
                        par = d2 % 2
                        w = [mstate["sp_xn"]]
                        if xnsl_war[par] is not None:
                            w.append(xnsl_war[par])
                        wout_pref[("ldn", d2)] = dma(
                            xnsl[:, par * TC:(par + 1) * TC],
                            bass.AP(xn_d.tensor, d2 * TC,
                                    [[ND * TC, 128], [1, TC]]),
                            semNL[par], waits=w)
                        xnsl_war[par] = None
                        w = [mstate["sp_x1"]]
                        if xsl_war[par] is not None:
                            w.append(xsl_war[par])
                        wout_pref[("lds", d2)] = dma(
                            xsl[:, par * TC:(par + 1) * TC],
                            bass.AP(x1_d.tensor, d2 * TC,
                                    [[ND * TC, 128], [1, TC]]),
                            semSL[par], waits=w)
                        xsl_war[par] = None

        def gen_ffn_C(c, stats_dep, pre_lds=None, zh=None):
            rs, mr = emit_ln_rows(False, from_rows=True, stats_dep=stats_dep)
            yield
            lds = dict(pre_lds) if pre_lds else {}
            for d in range(2):
                if d in lds:
                    continue
                w = [rs] if d == 0 else []
                if xsl_war[d % 2] is not None:
                    w.append(xsl_war[d % 2])
                lds[d] = dma(xsl[:, (d % 2) * TC:(d % 2 + 1) * TC],
                             bass.AP(x2_d.tensor, d * TC,
                                     [[ND * TC, 128], [1, TC]]),
                             semSL[d % 2], waits=w)
            xn = None
            for d in range(ND):
                par = d % 2
                wmul = [lds[d]] + ([rs, fence(semSP)] if d == 0 else [])
                if d == 0 and zh is not None and zh["mm"] is not None:
                    wmul.append(zh["mm"])
                sch.add("vector", lambda e, d=d, par=par: e.tensor_mul(
                    SCR[:, d * TC:(d + 1) * TC],
                    xsl[:, par * TC:(par + 1) * TC], repRS[:, :]),
                    waits=wmul, inc=sDVE)
                xn = sch.add("vector", lambda e, d=d: e.tensor_sub(
                    xn_sl(d), SCR[:, d * TC:(d + 1) * TC], repMR[:, :]),
                    waits=[mr] if d == 0 else [], inc=sDVE)
                xsl_war[par] = xn
                if d + 2 < ND:
                    lds[d + 2] = dma(xsl[:, par * TC:(par + 1) * TC],
                                     bass.AP(x2_d.tensor, (d + 2) * TC,
                                             [[ND * TC, 128], [1, TC]]),
                                     semSL[par], waits=[xn])
                if d % 2 == 1:
                    yield
            gelu_last = None
            gpend = []
            zspill_w = [fence(semZS)] if zh is not None else []
            for fp in range(NF // 2):
                slot = fp % 4
                ld, lo = wload(wblk(w1s, fp, 16), slot, 2048, None)
                m = None
                for h in range(2):
                    f = 2 * fp + h
                    b = 4 + (f % 4)

                    def mmg(e, lo=lo, h=h, b=b):
                        last = None
                        for k in range(8):
                            o = lo + (h * 8 + k) * 128
                            last = nc.tensor.matmul(
                                PS[b][:, :], wbuf[:, o: o + 128], xn_sl(k),
                                start=(k == 0), stop=(k == 7))
                        return last
                    m = sch.add("tensor", mmg, waits=pw(b, [ld, xn]), inc=sPE)
                    gpend.append((f, b, m))
                    while len(gpend) > 3:
                        f2, b2, m2 = gpend.pop(0)
                        gelu_last = sch.add(
                            "scalar", lambda e, f=f2, b=b2: e.activation(
                                ARENA[:, f * TC:(f + 1) * TC], PS[b][:, :],
                                AF.Gelu, bias=b1c[:, f:f + 1], scale=1.0),
                            waits=[m2] + zspill_w, inc=sACT)
                        zspill_w = []
                        ps_free[b2] = gelu_last
                wbuf_use(lo, lo + 2048, m)
                if fp % 2 == 1:
                    yield
            for f2, b2, m2 in gpend:
                gelu_last = sch.add("scalar", lambda e, f=f2, b=b2: e.activation(
                    ARENA[:, f * TC:(f + 1) * TC], PS[b][:, :], AF.Gelu,
                    bias=b1c[:, f:f + 1], scale=1.0), waits=[m2], inc=sACT)
                ps_free[b2] = gelu_last
            gpend = []
            yield
            pend = None

            def do_tail_c(d, a, lds2):
                par = d % 2
                spsl = sp[:, par * TC: par * TC + TC]
                res = sch.add("vector",
                              lambda e, par=par, spsl=spsl: e.scalar_tensor_tensor(
                                  xsl[:, par * TC:(par + 1) * TC], spsl, 0.5,
                                  xsl[:, par * TC:(par + 1) * TC],
                                  ALU.mult, ALU.add),
                              waits=[a, lds2], inc=sDVE)
                sp_war[par] = res
                xsl_war[par] = dma(
                    bass.AP(out_h.tensor, c * TC + d * 128 * L,
                            [[L, 128], [1, TC]]),
                    xsl[:, par * TC:(par + 1) * TC], semO, waits=[res])

            mm_q = []

            def do_evac_c(d, b, m):
                par = d % 2
                w = []
                if xsl_war[par] is not None:
                    w.append(xsl_war[par])
                lds2 = dma(xsl[:, par * TC:(par + 1) * TC],
                           bass.AP(x2_d.tensor, d * TC,
                                   [[ND * TC, 128], [1, TC]]),
                           semSL[par], waits=w)
                xsl_war[par] = None
                spsl = sp[:, par * TC: par * TC + TC]
                we = [m] + ([sp_war[par]] if sp_war[par] is not None else [])
                a = sch.add("scalar", lambda e, d=d, b=b, spsl=spsl: e.activation(
                    spsl, PS[b][:, :], AF.Identity, bias=b2c[:, d:d + 1],
                    scale=1.0), waits=we, inc=sACT)
                ps_free[b] = a
                return (d, a, lds2)

            for d in range(ND):
                ld0, lo0 = wload(
                    bass.AP(w2s.tensor, (d * 32) * 16384,
                            [[128, 128], [16384, 16], [1, 128]]),
                    (2 * d) % 4, 2048, None)
                ld1, lo1 = wload(
                    bass.AP(w2s.tensor, (d * 32 + 16) * 16384,
                            [[128, 128], [16384, 16], [1, 128]]),
                    (2 * d + 1) % 4, 2048, None)
                b = 6 + (d % 2)

                if len(mm_q) > 1:
                    if pend is not None:
                        do_tail_c(*pend)
                    pend = do_evac_c(*mm_q.pop(0))

                def mmg2(e, lo0=lo0, lo1=lo1, b=b):
                    last = None
                    for k in range(32):
                        lo = lo0 if k < 16 else lo1
                        o = lo + (k % 16) * 128
                        last = nc.tensor.matmul(
                            PS[b][:, :], wbuf[:, o: o + 128],
                            ARENA[:, k * TC:(k + 1) * TC],
                            start=(k == 0), stop=(k == 31))
                    return last
                m = sch.add("tensor", mmg2, waits=pw(b, [ld0, ld1, gelu_last]),
                            inc=sPE)
                wbuf_use(lo0, lo0 + 2048, m)
                wbuf_use(lo1, lo1 + 2048, m)
                mm_q.append((d, b, m))
                yield
            while mm_q:
                if pend is not None:
                    do_tail_c(*pend)
                pend = do_evac_c(*mm_q.pop(0))
            do_tail_c(*pend)

        def emit_m1(c, x_ready):
            rs, mr = emit_ln_rows(True, stats_dep=None)
            xn = emit_ln_apply_X(True, rs, mr)
            pair_order = list(range(8))
            for jp, blk in enumerate(pair_order):
                slot = jp % 4
                ld, lo = wload(wblk(wins, blk, 16), slot, 2048,
                               wins_ready if jp < 4 else None)
                m = None
                for h in range(2):
                    f = 2 * blk + h
                    b = (2 * jp + h) % 4
                    def mmg(e, lo=lo, h=h, b=b):
                        last = None
                        for k in range(8):
                            o = lo + (h * 8 + k) * 128
                            last = nc.tensor.matmul(
                                PS[b][:, :], wbuf[:, o: o + 128], xn_sl(k),
                                start=(k == 0), stop=(k == 7))
                        return last
                    m = sch.add("tensor", mmg, waits=pw(b, [ld, xn]), inc=sPE)
                    if f >= 16:
                        ei = f - 16
                        a = sch.add("scalar", lambda e, ei=ei, b=b: e.copy(
                            ARENA[:, ei * TC:(ei + 1) * TC], PS[b][:, :]),
                            waits=[m], inc=sACT)
                    else:
                        ei = f
                        a = sch.add("scalar", lambda e, ei=ei, b=b: e.copy(
                            XSYG[:, ei * (TC + 3) + 3: (ei + 1) * (TC + 3)],
                            PS[b][:, :]), waits=[m], inc=sACT)
                    ps_free[b] = a
                wbuf_use(lo, lo + 2048, m)
            sp_z = (semZS, 16 * (c + 1))
            sp_xn = dma(bass.AP(xn_d.tensor, 0, [[ND * TC, 128], [1, ND * TC]]),
                        SCR[:, ND * TC: 2 * ND * TC], semSP,
                        waits=[xn, fence(semNL[0]), fence(semNL[1])])
            sp_x1 = dma(bass.AP(x1_d.tensor, 0, [[ND * TC, 128], [1, ND * TC]]),
                        X[:, :], semSP,
                        waits=[(sDVE, sDVE.v), fence(semSL[0]), fence(semSL[1])])
            hr = None
            for ei in range(NE):
                hr = sch.add("scalar", lambda e, ei=ei: e.copy(
                    XSYG[:, ei * (TC + 3): ei * (TC + 3) + 3],
                    halos[:, ei * 3:(ei + 1) * 3]), inc=sACT)
            xs_ready = hr
            u_last = None
            for ei in range(NE):
                cp = ei % 2
                wcd = [cds_ready] if ei < 2 else []
                if cdsl_war[cp] is not None:
                    wcd.append(cdsl_war[cp])
                cld = dma(cdsl[:, cp * 512:(cp + 1) * 512],
                          bass.AP(cds.tensor, ei * 4 * 16384,
                                  [[128, 128], [16384, 4], [1, 128]]),
                          semC[cp], waits=wcd)
                b = 2 + ei % 2
                def convg(e, ei=ei, b=b, cp=cp):
                    last = None
                    for tap in range(4):
                        last = nc.tensor.matmul(
                            PS[b][:, :],
                            cdsl[:, cp * 512 + tap * 128: cp * 512 + (tap + 1) * 128],
                            XSYG[:, ei * (TC + 3) + tap: ei * (TC + 3) + tap + TC],
                            start=(tap == 0), stop=(tap == 3))
                    return last
                m = sch.add("tensor", convg, waits=pw(b, [xs_ready, cld]), inc=sPE)
                cdsl_war[cp] = m
                a = sch.add("scalar", lambda e, ei=ei, b=b: e.activation(
                    u_sl(ei), PS[b][:, :], AF.Silu, bias=cbc[:, ei:ei + 1],
                    scale=1.0), waits=[m], inc=sACT)
                ps_free[b] = a
                u_last = sch.add("scalar", lambda e, ei=ei: e.copy(
                    halos[:, ei * 3:(ei + 1) * 3],
                    XSYG[:, ei * (TC + 3) + TC: ei * (TC + 3) + TC + 3]),
                    inc=sACT)
            u_ready = u_last

            def xprojg(e):
                last = None
                for k in range(NE):
                    last = nc.tensor.matmul(
                        PS[5][0:64, :], wxp[:, k * 96: k * 96 + 64], u_sl(k),
                        start=(k == 0), stop=(k == NE - 1))
                for k in range(NE):
                    last = nc.tensor.matmul(
                        PS[4][0:32, :], wxp[:, k * 96 + 64: k * 96 + 96], u_sl(k),
                        start=(k == 0), stop=(k == NE - 1))
                return last
            m = sch.add("tensor", xprojg, waits=pw(5, pw(4, [u_ready])), inc=sPE)
            a1 = sch.add("scalar", lambda e: e.copy(dtrbf[:, :], PS[5][0:64, :]),
                         waits=[m], inc=sACT)
            ps_free[5] = a1
            wbc = [m]
            if bcw_prev[0] is not None:
                wbc.append(bcw_prev[0])
            a2 = sch.add("scalar", lambda e: e.copy(bcbf[:, :], PS[4][0:32, :]),
                         waits=wbc, inc=sACT)
            ps_free[4] = a2
            bcw = dma(bass.AP(bc_dram.tensor, 0, [[TC, 2], [2 * TC, 16], [1, TC]]),
                      bcbf[:, :], semSP, waits=[a2])
            bcw_prev[0] = bcw
            dt_dones, dtu_dones = {}, {}
            for ei in range(NE):
                b = 2 + ei % 2
                m = sch.add("tensor", lambda e, ei=ei, b=b: e.matmul(
                    PS[b][:, :], wdt[:, ei * 128:(ei + 1) * 128], dtrbf[:, :],
                    start=True, stop=True), waits=pw(b, [a1]), inc=sPE)
                par = ei % 2
                spsl = sp[:, par * TC: par * TC + TC]
                we = [m] + ([sp_war[par]] if sp_war[par] is not None else [])
                a = sch.add("scalar", lambda e, spsl=spsl, ei=ei, b=b: e.activation(
                    spsl, PS[b][:, :], AF.Exp, bias=bdtc[:, ei:ei + 1],
                    scale=1.0), waits=we, inc=sACT)
                ps_free[b] = a
                a = sch.add("scalar", lambda e, spsl=spsl, ei=ei: e.activation(
                    dtbf[:, ei * TC:(ei + 1) * TC], spsl, AF.Ln,
                    bias=consts[:, 1:2], scale=1.0), inc=sACT)
                sp_war[par] = a
                dt_dones[ei] = a
                dtu_dones[ei] = sch.add("vector", lambda e, ei=ei: e.tensor_mul(
                    dtu[:, ei * TC:(ei + 1) * TC], dtbf[:, ei * TC:(ei + 1) * TC],
                    u_sl(ei)), waits=[a], inc=sDVE)
            sp_u = dma(bass.AP(u_d.tensor, 0, [[NE * TC, 128], [1, NE * TC]]),
                       ARENA[:, 8192:16384], semSP,
                       waits=[(sPE, sPE.v), (sDVE, sDVE.v)])
            return dict(dt=dt_dones, dtu=dtu_dones, bcw=bcw, u_ready=u_ready,
                        sp_x1=sp_x1, sp_xn=sp_xn, sp_z=sp_z, sp_u=sp_u,
                        xn_f=xn)

        def emit_scan(c, m1, pumps):
            LA = 2
            bc_free = {}
            bc_ready = {}
            gate_done = {}
            dA_done = {}
            dA_war = {}
            h_war = {}
            tmp_war = {0: None, 1: None}
            last_idacc = {}
            dt_dones = m1["dt"]
            dtu_dones = m1["dtu"]
            bcw = m1["bcw"]
            u_ready = m1["u_ready"]

            def bcast_pair(np5, half, cnt):
                return bass.AP(BC, (2 * np5 + half) * TC,
                               [[10 * TC, 128], [0, cnt], [1, TC]])

            def pump():
                for g in pumps:
                    try:
                        next(g)
                        return
                    except StopIteration:
                        continue

            tg0 = 0
            zload_done = {}
            for gi, g in enumerate(GROUPS):
                ng = len(g)
                for bi, e_ in enumerate(g):
                    bank = bi
                    cp = e_ % 2
                    wu = [m1["sp_u"]]
                    if cdsl_war[cp] is not None:
                        wu.append(cdsl_war[cp])
                    ul = dma(cdsl[:, cp * 512:(cp + 1) * 512],
                             bass.AP(u_d.tensor, e_ * TC,
                                     [[NE * TC, 128], [1, TC]]),
                             semC[cp], waits=wu)
                    mu_ = sch.add("tensor", lambda e, e_=e_, bank=bank, cp=cp: e.matmul(
                        PS[bank][:, :], ddiag[:, e_ * 128:(e_ + 1) * 128],
                        cdsl[:, cp * 512:(cp + 1) * 512],
                        start=True, stop=False, skip_group_check=True),
                        waits=pw(bank, [ul]), inc=sPE)
                    cdsl_war[cp] = mu_
                zls = {}

                def emit_zls(g=g):
                    for bi, e_ in enumerate(g[:2]):
                        par = e_ % 2
                        wz = [m1["sp_z"]]
                        if cdsl_war[par] is not None:
                            wz.append(cdsl_war[par])
                        zls[e_] = dma(cdsl[:, par * 512:(par + 1) * 512],
                                      bass.AP(z_d.tensor, e_ * TC,
                                              [[NE * TC, 128], [1, TC]]),
                                      semZL[par], waits=wz)
                        cdsl_war[par] = None
                if gi > 0:
                    emit_zls()
                seq = [(n, i) for n in range(NST) for i in range(ng)]
                for si in range(len(seq) + LA):
                    if gi == 0 and si == 12 * ng:
                        emit_zls()
                    if si < len(seq):
                        n, i = seq[si]
                        if i == 0:
                            np5 = n % 5
                            wb = [bcw]
                            if np5 in bc_free:
                                wb.append(bc_free[np5])
                            bc_ready[n] = dma(
                                BC[:, 2 * np5 * TC: (2 * np5 + 2) * TC],
                                b_bcpair(n), semB[np5], waits=wb)
                        tg = tg0 + si
                        tp = tg % 3
                        wa = [dt_dones[g[i]]]
                        if tp in dA_war:
                            wa.append(dA_war[tp])
                        dA_done[tg] = sch.add(
                            "scalar", lambda e, e_=g[i], n=n, tp=tp: e.activation(
                                dA2[:, tp * TC:(tp + 1) * TC],
                                dtbf[:, e_ * TC:(e_ + 1) * TC], AF.Exp,
                                bias=0.0,
                                scale=aneg[:, e_ * NST + n: e_ * NST + n + 1]),
                            waits=wa, inc=sACT)
                    if si >= LA:
                        n, i = seq[si - LA]
                        tg = tg0 + si - LA
                        tp = tg % 3
                        e_ = g[i]
                        idx = e_ * NST + n
                        np5 = n % 5
                        if i == 0:
                            gb = g[0] * TC
                            sch.add("vector",
                                    lambda e, gb=gb, np5=np5, ng=ng: e.tensor_tensor(
                                        dBu[:, 0:ng * TC], dtu[:, gb: gb + ng * TC],
                                        bcast_pair(np5, 0, ng), ALU.mult),
                                    waits=[bc_ready[n], dtu_dones[g[ng - 1]]],
                                    inc=sDVE)
                        ws = [dA_done[tg]]
                        if i in h_war:
                            ws.append(h_war[i])
                        sc = sch.add(
                            "vector", lambda e, tp=tp, i=i, idx=idx: e.tensor_tensor_scan(
                                hbuf[:, i * TC:(i + 1) * TC],
                                dA2[:, tp * TC:(tp + 1) * TC],
                                dBu[:, i * TC:(i + 1) * TC],
                                hstate[:, idx:idx + 1],
                                ALU.mult, ALU.add), waits=ws, inc=sDVE)
                        dA_war[tp] = sc
                        if i == ng - 1:
                            bidx = g[0] * NST + n
                            cy = sch.add(
                                "scalar",
                                lambda e, bidx=bidx, ng=ng: e.copy(
                                    bass.AP(hstate, bidx,
                                            [[NE * NST, 128], [NST, ng]]),
                                    bass.AP(hbuf, TC - 1,
                                            [[ng * TC, 128], [TC, ng]])),
                                waits=[sc], inc=sACT)
                            for i2 in range(ng):
                                h_war[i2] = cy
                        if i == ng - 1:
                            rp = n % 2
                            ybuf = tmpb if rp == 0 else XSYG
                            yb = 0 if rp == 0 else 12 * TC
                            wy = []
                            if tmp_war[rp] is not None:
                                wy.append(tmp_war[rp])
                            ym = sch.add(
                                "vector",
                                lambda e, np5=np5, ng=ng, ybuf=ybuf, yb=yb:
                                e.tensor_tensor(
                                    ybuf[:, yb:yb + ng * TC], hbuf[:, 0:ng * TC],
                                    bcast_pair(np5, 1, ng), ALU.mult),
                                waits=wy, inc=sDVE)
                            bc_free[np5] = ym
                            mmi = None
                            for bi2, e2 in enumerate(g):
                                mmi = sch.add(
                                    "tensor",
                                    lambda e, bi2=bi2, n=n, ybuf=ybuf, yb=yb:
                                    e.matmul(
                                        PS[bi2][:, :], eye[:, :],
                                        ybuf[:, yb + bi2 * TC:yb + (bi2 + 1) * TC],
                                        start=False, stop=(n == NST - 1),
                                        skip_group_check=True),
                                    waits=[ym] if bi2 == 0 else [], inc=sPE)
                                last_idacc[e2] = mmi
                            tmp_war[rp] = mmi
                            pump()
                tg0 += len(seq)
                for bi, e_ in enumerate(g):
                    par = e_ % 2
                    if e_ in zls:
                        zl = zls[e_]
                    else:
                        wz = [m1["sp_z"]]
                        if cdsl_war[par] is not None:
                            wz.append(cdsl_war[par])
                        zl = dma(cdsl[:, par * 512:(par + 1) * 512],
                                 bass.AP(z_d.tensor, e_ * TC,
                                         [[NE * TC, 128], [1, TC]]),
                                 semZL[par], waits=wz)
                    wsil = [zl]
                    if (e_ - 2) in gate_done:
                        wsil.append(gate_done[e_ - 2])
                    az = sch.add("scalar", lambda e, par=par: e.activation(
                        zq[:, par * TC:(par + 1) * TC],
                        cdsl[:, par * 512:(par + 1) * 512], AF.Silu,
                        bias=0.0, scale=1.0), waits=wsil, inc=sACT)
                    gv = sch.add(
                        "vector", lambda e, e_=e_, bi=bi, par=par: e.tensor_mul(
                            XSYG[:, e_ * TC:(e_ + 1) * TC], PS[bi][:, :],
                            zq[:, par * TC:(par + 1) * TC]),
                        waits=[az, last_idacc[e_]], inc=sDVE)
                    gate_done[e_] = gv
                    cdsl_war[par] = gv
                    ps_free[bi] = gv
            return (sDVE, sDVE.v)

        def emit_m2(c, m1, gates, pref=None):
            for pos, d in enumerate([0, 1, 4, 5, 2, 3, 6, 7]):
                slot = d % 4
                if pref is not None and d in pref:
                    ld, lo = pref[d]
                else:
                    ld, lo = wload(wblk(wouts, d, 16), slot, 2048,
                                   wouts_ready if d < 4 else None)
                lane = d % 4
                if lane < 2:
                    xnv, xnw = xnsl[:, lane * TC:(lane + 1) * TC], xnsl_war
                    x1v, x1w = xsl[:, lane * TC:(lane + 1) * TC], xsl_war
                    lk = lane
                else:
                    lk = lane - 2
                    xnv, xnw = cdsl[:, lk * 512:(lk + 1) * 512], cdsl_war
                    x1v, x1w = sp[:, lk * TC:(lk + 1) * TC], sp_war
                if pref is not None and ("ldn", d) in pref:
                    ldn = pref[("ldn", d)]
                else:
                    ldw = [m1["sp_xn"]]
                    if lane >= 2 and d < 4:
                        ldw.append(gates)
                    if xnw[lk] is not None:
                        ldw.append(xnw[lk])
                    ldn = dma(xnv, bass.AP(xn_d.tensor, d * TC,
                                           [[ND * TC, 128], [1, TC]]),
                              semNL[lane % 2], waits=ldw)
                if pref is not None and ("lds", d) in pref:
                    lds = pref[("lds", d)]
                else:
                    ldw = [m1["sp_x1"]]
                    if x1w[lk] is not None:
                        ldw.append(x1w[lk])
                    lds = dma(x1v, bass.AP(x1_d.tensor, d * TC,
                                           [[ND * TC, 128], [1, TC]]),
                              semSL[lane % 2], waits=ldw)
                b = d % 2
                def mmg3(e, lo=lo, b=b, xnv=xnv):
                    for k in range(NE):
                        nc.tensor.matmul(
                            PS[b][:, :], wbuf[:, lo + k * 128: lo + (k + 1) * 128],
                            XSYG[:, k * TC:(k + 1) * TC],
                            start=(k == 0), stop=False)
                    return nc.tensor.matmul(
                        PS[b][:, :], eye[:, :], xnv,
                        start=False, stop=True)
                m = sch.add("tensor", mmg3, waits=pw(b, [ld, ldn, gates]), inc=sPE)
                wbuf_use(lo, lo + 2048, m)
                xnw[lk] = m
                res = sch.add("vector", lambda e, x1v=x1v, b=b: e.tensor_add(
                    x1v, x1v, PS[b][:, :]),
                    waits=[m, lds], inc=sDVE)
                ps_free[b] = res
                smm = emit_stats_d(d, res, src=x1v, bx=6, bq=7, idx=pos)
                st = dma(bass.AP(x2_d.tensor, d * TC,
                                 [[ND * TC, 128], [1, TC]]),
                         x1v, semSP, waits=[res])
                x1w[lk] = st
            if pref is not None:
                pref["done"] = True
            clds = {}
            for d in range(2):
                par = d % 2
                w = []
                if xsl_war[par] is not None:
                    w.append(xsl_war[par])
                clds[d] = dma(xsl[:, par * TC:(par + 1) * TC],
                              bass.AP(x2_d.tensor, d * TC,
                                      [[ND * TC, 128], [1, TC]]),
                              semSL[par], waits=w)
                xsl_war[par] = None
            return emit_stats_evac(smm), clds

        prev_gates = None
        m1_state = None
        wp_prev = None
        clds_prev = None
        stats_x2_dep = None
        genA_next = None
        xin_prev = None
        for c in range(NCH):
            if c == 0:
                w = [setup_a, setup_v]
                xin = dma(X[:, :], _xio(x_h, 0), semX, waits=w)
                for d in range(ND):
                    emit_stats_d(d, xin)
                gA = gen_ffn_A(0, xin)
                for _ in gA:
                    pass
                r1 = gen_ffn_A.res
            else:
                stats_x2_dep, clds_prev = emit_m2(c - 1, m1_state, prev_gates,
                                                  pref=wp_prev)
                wp_prev = None
                for _ in genA_next:
                    pass
                r1 = gen_ffn_A.res
            m1_state = emit_m1(c, r1)
            zh_gen = gen_zh(c, prev_gates, m1_state["xn_f"])
            pumps = [zh_gen]
            genC_cur = None
            if c > 0:
                genC_cur = gen_ffn_C(c - 1, stats_x2_dep, pre_lds=clds_prev,
                                     zh=zh_state)
                pumps.append(genC_cur)
            if c + 1 < NCH:
                xin = dma(X[:, :], _xio(x_h, c + 1), semX,
                          waits=[m1_state["sp_x1"], (sACT, sACT.v)])
                wp_cur = {}

                def genA_wrap(c=c, xin=xin, wp=wp_cur, ms=m1_state):
                    for d in range(ND):
                        emit_stats_d(d, xin)
                    yield
                    yield from gen_ffn_A(c + 1, xin, wout_pref=wp, mstate=ms)
                genA_next = genA_wrap()
                wp_prev = wp_cur
                import os as _os
                if _os.environ.get("NO_APUMP") != "1":
                    pumps.append(genA_next)
            prev_gates = emit_scan(c, m1_state, pumps)
            for _ in zh_gen:
                pass
            if genC_cur is not None:
                for _ in genC_cur:
                    pass
        stats_x2_dep, clds_prev = emit_m2(NCH - 1, m1_state, prev_gates,
                                          pref=wp_prev)
        for _ in gen_ffn_C(NCH - 1, stats_x2_dep, pre_lds=clds_prev,
                           zh=zh_state):
            pass
        sch.add("sync", lambda e: e.nop(), waits=[fence(semO)])

        with nc.Block() as block:
            sch.emit(block)

    return nc


def _prep_shared(conv_w, conv_b, w_xproj, w_dt, b_dt, a_log, d_skip, w_out,
                 ln_w, ln_b, W1, b1, W2, b2, w_in):
    bf = ml_dtypes.bfloat16
    d = {}
    d["w1blk"] = np.ascontiguousarray(
        W1.T.reshape(8, 128, NF, 128).transpose(2, 0, 1, 3))
    d["w2blk"] = np.ascontiguousarray(
        W2.T.reshape(32, 128, ND, 128).transpose(2, 0, 1, 3))
    d["winblk"] = np.ascontiguousarray(
        w_in.T.reshape(8, 128, NF, 128).transpose(2, 0, 1, 3))
    d["woutblk"] = np.ascontiguousarray(
        w_out.T.reshape(16, 128, ND, 128).transpose(2, 0, 1, 3))
    d["wxpblk"] = np.ascontiguousarray(w_xproj.T.reshape(16, 128, 96))
    d["wdtblk"] = np.ascontiguousarray(
        w_dt.T.reshape(64, 16, 128).transpose(1, 0, 2))
    idx = np.arange(128)
    cd = np.zeros((NE, 4, 128, 128), np.float32)
    for e_ in range(NE):
        for tap in range(4):
            cd[e_, tap, idx, idx] = conv_w[e_ * 128:(e_ + 1) * 128, tap]
    d["cdiag"] = cd
    dd = np.zeros((128, NE * 128), np.float32)
    for e_ in range(NE):
        dd[idx, e_ * 128 + idx] = d_skip[e_ * 128:(e_ + 1) * 128]
    d["ddiag"] = dd
    d["alogc"] = np.ascontiguousarray(
        a_log.reshape(NE, 128, NST).transpose(1, 0, 2).reshape(128, NE * NST))
    d["b1c"] = np.ascontiguousarray(b1.reshape(NF, 128).T)
    d["b2c"] = np.ascontiguousarray(b2.reshape(ND, 128).T)
    d["cbc"] = np.ascontiguousarray(conv_b.reshape(NE, 128).T)
    d["bdtc"] = np.ascontiguousarray(b_dt.reshape(NE, 128).T)
    d["lnwc"] = np.ascontiguousarray(ln_w.reshape(ND, 128).T)
    d["lnbc"] = np.ascontiguousarray(ln_b.reshape(ND, 128).T)
    d["eye128"] = np.eye(128, dtype=bf)
    d["onescol"] = np.ones((128, 1), np.float32)
    d["onescolb"] = np.ones((128, 1), dtype=bf)
    return d


def kernel(L0_attn_x, L0_attn_ln_w, L0_attn_ln_b, L0_attn_w_in, L0_attn_conv_w,
           L0_attn_conv_b, L0_attn_w_xproj, L0_attn_w_dt, L0_attn_b_dt,
           L0_attn_a_log, L0_attn_d_skip, L0_attn_w_out, L0_ffn_W1, L0_ffn_b1,
           L0_ffn_W2, L0_ffn_b2):
    x = np.asarray(L0_attn_x, np.float32)
    B, L, _ = x.shape
    nc = build_nc(L)
    shared = _prep_shared(
        np.asarray(L0_attn_conv_w, np.float32),
        np.asarray(L0_attn_conv_b, np.float32),
        np.asarray(L0_attn_w_xproj, np.float32),
        np.asarray(L0_attn_w_dt, np.float32),
        np.asarray(L0_attn_b_dt, np.float32),
        np.asarray(L0_attn_a_log, np.float32),
        np.asarray(L0_attn_d_skip, np.float32),
        np.asarray(L0_attn_w_out, np.float32),
        np.asarray(L0_attn_ln_w, np.float32),
        np.asarray(L0_attn_ln_b, np.float32),
        np.asarray(L0_ffn_W1, np.float32),
        np.asarray(L0_ffn_b1, np.float32),
        np.asarray(L0_ffn_W2, np.float32),
        np.asarray(L0_ffn_b2, np.float32),
        np.asarray(L0_attn_w_in, np.float32))
    in_maps = []
    for b in range(B):
        m = dict(shared)
        m["x"] = np.ascontiguousarray(x[b].T)
        in_maps.append(m)
    res = run_bass_kernel_spmd(nc, in_maps, core_ids=list(range(B)))
    out = np.stack([np.asarray(r["out"]).T for r in res.results], axis=0)
    return np.ascontiguousarray(out.astype(np.float32))


# version 33
# speedup vs baseline: 1.0191x; 1.0011x over previous
import contextlib
import numpy as np
import ml_dtypes
import concourse.bass as bass
import concourse.mybir as mybir
from concourse.bass_utils import run_bass_kernel_spmd

F32 = mybir.dt.float32
BF16 = mybir.dt.bfloat16
AF = mybir.ActivationFunctionType
ALU = mybir.AluOpType

D = 1024
FF = 4096
E = 2048
NST = 16
TC = 512
ND = D // 128
NF = FF // 128
NE = E // 128
EPS = 1e-5
INV_D = 1.0 / D
ENGINES = ("sync", "tensor", "scalar", "vector", "gpsimd")
GROUPS = [list(range(4 * i, 4 * i + 4)) for i in range(4)]


class Sem:
    __slots__ = ("h", "v", "name")


class Sched:
    def __init__(self, nc, st):
        self.nc = nc
        self.st = st
        self.items = {e: [] for e in ENGINES}

    def sem(self, name):
        s = Sem()
        s.h = self.st.enter_context(self.nc.semaphore(name))
        s.v = 0
        s.name = name
        return s

    def add(self, eng, fn, waits=(), inc=None, amt=1):
        w = [x for x in waits if x is not None]
        self.items[eng].append((tuple(w), fn, inc, amt))
        if inc is not None:
            inc.v += amt
            return (inc, inc.v)
        return None

    def emit(self, block):
        for eng in ENGINES:
            items = self.items[eng]
            if not items:
                continue
            section = getattr(block, eng)

            def make(items=items):
                def _(engobj):
                    last = {}
                    for waits, fn, inc, amt in items:
                        for (s, v) in waits:
                            if v > last.get(id(s), 0):
                                engobj.wait_ge(s.h, v)
                                last[id(s)] = v
                        inst = fn(engobj)
                        if inc is not None:
                            inst.then_inc(inc.h, amt)
                return _
            section(make())


def build_nc(L):
    NCH = L // TC
    nc = bass.Bass("TRN2", target_bir_lowering=False,
                   detect_race_conditions=False)

    di = lambda n, s, d: nc.dram_tensor(n, s, d, kind="ExternalInput").ap()
    x_h = di("x", [D, L], F32)
    w1_h = di("w1blk", [NF, 8, 128, 128], F32)
    w2_h = di("w2blk", [ND, 32, 128, 128], F32)
    win_h = di("winblk", [NF, 8, 128, 128], F32)
    wout_h = di("woutblk", [ND, 16, 128, 128], F32)
    wxp_h = di("wxpblk", [16, 128, 96], F32)
    wdt_h = di("wdtblk", [16, 64, 128], F32)
    cdiag_h = di("cdiag", [NE, 4, 128, 128], F32)
    ddiag_h = di("ddiag", [128, NE * 128], F32)
    alog_h = di("alogc", [128, NE * NST], F32)
    b1c_h = di("b1c", [128, NF], F32)
    b2c_h = di("b2c", [128, ND], F32)
    cbc_h = di("cbc", [128, NE], F32)
    bdtc_h = di("bdtc", [128, NE], F32)
    lnwc_h = di("lnwc", [128, ND], F32)
    lnbc_h = di("lnbc", [128, ND], F32)
    eye_h = di("eye128", [128, 128], BF16)
    onescol_h = di("onescol", [128, 1], F32)
    onescolb_h = di("onescolb", [128, 1], BF16)

    out_h = nc.dram_tensor("out", [D, L], F32, kind="ExternalOutput").ap()

    ds = lambda n, s, d: nc.dram_tensor(n, s, d, kind="Internal").ap()
    w1s = ds("w1s", [NF, 8, 128, 128], BF16)
    w2s = ds("w2s", [ND, 32, 128, 128], BF16)
    wins = ds("wins", [NF, 8, 128, 128], BF16)
    wouts = ds("wouts", [ND, 16, 128, 128], BF16)
    cds = ds("cds", [NE, 4, 128, 128], BF16)
    bc_dram = ds("bc_dram", [32, TC], BF16)
    x1_d = ds("x1_d", [128, ND * TC], F32)
    x2_d = ds("x2_d", [128, ND * TC], F32)
    xn_d = ds("xn_d", [128, ND * TC], BF16)
    z_d = ds("z_d", [128, NE * TC], BF16)
    u_d = ds("u_d", [128, NE * TC], BF16)

    def wblk(ap, i, kc):
        return bass.AP(ap.tensor, i * kc * 16384,
                       [[128, 128], [16384, kc], [1, 128]])

    def _xio(ap, c):
        return bass.AP(ap.tensor, c * TC, [[L, 128], [128 * L, ND], [1, TC]])

    def b_bcpair(n):
        return bass.AP(bc_dram.tensor, 2 * n * TC, [[0, 128], [TC, 2], [1, TC]])

    with contextlib.ExitStack() as st:
        sb = lambda n, s, d: st.enter_context(nc.sbuf_tensor(n, s, d))
        X = sb("X", [128, ND * TC], F32)
        dA2 = sb("dA2", [128, 3 * TC], F32)
        hstate = sb("hstate", [128, NE * NST], F32)
        aneg = sb("aneg", [128, NE * NST], F32)
        thrs = sb("thrs", [128, 4], F32)
        absacc = sb("absacc", [128, 64], F32)
        repRS = sb("repRS", [128, TC], F32)
        repMR = sb("repMR", [128, TC], BF16)
        sp = sb("sp", [128, 2 * TC], F32)
        rowf = sb("rowf", [1, 6 * TC], F32)
        xsl = sb("xsl", [128, 2 * TC], F32)
        zq = sb("zq", [128, 2 * TC], F32)
        xnsl = sb("xnsl", [128, 2 * TC], BF16)
        b1c = sb("b1c_s", [128, NF], F32)
        b2c = sb("b2c_s", [128, ND], F32)
        cbc = sb("cbc_s", [128, NE], F32)
        bdtc = sb("bdtc_s", [128, NE], F32)
        lnwc = sb("lnwc_s", [128, ND], F32)
        lnbc = sb("lnbc_s", [128, ND], F32)
        onescol = sb("onescol_s", [128, 1], F32)
        onescolb = sb("onescolb_s", [128, 1], BF16)
        ones1x128 = sb("ones1x128_s", [1, 128], F32)
        consts = sb("consts_s", [128, 2], F32)

        SCR = sb("SCR", [128, 2 * ND * TC], BF16)
        ARENA = sb("ARENA", [128, NF * TC], BF16)
        dtbf = sb("dtbf", [128, NE * TC], BF16)
        XSYG = sb("XSYG", [128, NE * (TC + 3)], BF16)
        dtu = sb("dtu", [128, NE * TC], BF16)
        wbuf = sb("wbuf", [128, 8192], BF16)
        BC = sb("BC", [128, 10 * TC], BF16)
        dBu = sb("dBu", [128, 4 * TC], BF16)
        hbuf = sb("hbuf", [128, 4 * TC], BF16)
        tmpb = sb("tmpb", [128, 4 * TC], BF16)
        xsq = sb("xsq", [128, 2 * TC], BF16)
        cdsl = sb("cdsl", [128, 2 * 512], BF16)
        ddiag = sb("ddiag_s", [128, NE * 128], BF16)
        wxp = sb("wxp_s", [128, 16 * 96], BF16)
        wdt = sb("wdt_s", [64, 16 * 128], BF16)
        eye = sb("eye_s", [128, 128], BF16)
        bcbf = sb("bcbf", [32, TC], BF16)
        dtrbf = sb("dtrbf", [64, TC], BF16)
        halos = sb("halos", [128, NE * 3], BF16)
        PS = [st.enter_context(nc.psum_tensor(f"ps{i}", [128, TC], F32))
              for i in range(8)]

        sch = Sched(nc, st)
        sPE = sch.sem("sPE")
        sACT = sch.sem("sACT")
        sDVE = sch.sem("sDVE")
        sGP = sch.sem("sGP")
        semX = sch.sem("semX")
        semO = sch.sem("semO")
        semW = [sch.sem(f"semW{i}") for i in range(4)]
        semB = [sch.sem(f"semB{i}") for i in range(5)]
        semZS = sch.sem("semZS")
        zh_state = {"mm": None}
        semC = [sch.sem(f"semC{i}") for i in range(2)]
        semSL = [sch.sem(f"semSL{i}") for i in range(2)]
        semNL = [sch.sem(f"semNL{i}") for i in range(2)]
        semZL = [sch.sem(f"semZL{i}") for i in range(2)]
        semSP = sch.sem("semSP")
        semSCR = sch.sem("semSCR")
        semS = [sch.sem(f"semS{i}") for i in range(4)]
        semMISC = sch.sem("semMISC")

        def fence(s):
            return (s, s.v)

        ps_free = {i: None for i in range(8)}
        wbuf_deps = []
        sp_war = {0: None, 1: None}
        bcw_prev = [None]
        xsl_war = {0: None, 1: None}
        cdsl_war = {0: None, 1: None}
        xnsl_war = {0: None, 1: None}
        zsl_war = {0: None, 1: None}
        sq_war = {0: None, 1: None}
        sq_idx = [0]

        def wbuf_claim(lo, hi):
            w, keep = [], []
            for ent in wbuf_deps:
                if ent[0] < hi and lo < ent[1]:
                    w.append(ent[2])
                else:
                    keep.append(ent)
            wbuf_deps[:] = keep
            return w

        def wbuf_use(lo, hi, dep):
            wbuf_deps.append((lo, hi, dep))

        def pw(bank, extra=()):
            w = list(extra)
            if ps_free[bank] is not None:
                w.append(ps_free[bank])
            return w

        def dma(dst, src, sem, waits=()):
            sch.add("sync", lambda e, d=dst, s=src: e.dma_start(d, s),
                    waits=waits, inc=sem, amt=16)
            return fence(sem)

        for dst, src in ((b1c, b1c_h), (b2c, b2c_h), (cbc, cbc_h), (bdtc, bdtc_h),
                         (lnwc, lnwc_h), (lnbc, lnbc_h), (onescol, onescol_h),
                         (onescolb, onescolb_h), (eye, eye_h)):
            dma(dst[:, :], src, semMISC)
        sch.add("vector", lambda e: e.memset(hstate[:, :], 0.0), inc=sDVE)
        sch.add("vector", lambda e: e.memset(halos[:, :], 0.0), inc=sDVE)
        sch.add("vector", lambda e: e.memset(consts[:, 0:1], EPS), inc=sDVE)
        sch.add("vector", lambda e: e.memset(consts[:, 1:2], 1.0), inc=sDVE)
        sch.add("vector", lambda e: e.memset(ones1x128[0:1, :], 1.0), inc=sDVE)

        def w_mean_pass(v_src, nblk, width, acc_off):
            acts = []
            ns = 4096 // width
            for i in range(nblk):
                par = i % ns
                w = [acts[i - ns]] if i >= ns else [(sDVE, sDVE.v)]
                ld = dma(X[:, par * width: par * width + width], v_src(i),
                         semS[par], waits=w)
                a = sch.add("scalar", lambda e, p=par, wd=width, i=i: e.activation(
                    SCR[:, p * wd: p * wd + wd], X[:, p * wd: p * wd + wd],
                    AF.Abs, bias=0.0, scale=1.0,
                    accum_out=absacc[:, acc_off + i: acc_off + i + 1]),
                    waits=[ld], inc=sACT)
                acts.append(a)
            return acts[-1]

        def scale_math(off, n, ti, nelem, prev):
            v = sch.add("vector", lambda e, off=off, n=n: e.tensor_reduce(
                absacc[:, 63:64], absacc[:, off:off + n],
                mybir.AxisListType.X, ALU.add), waits=[prev], inc=sDVE)
            b = 6
            m = sch.add("tensor", lambda e: e.matmul(
                PS[b][0:1, 0:1], onescol[:, 0:1], absacc[:, 63:64],
                start=True, stop=True),
                waits=pw(b, [v, fence(semMISC)]), inc=sPE)
            a = sch.add("scalar", lambda e, nelem=nelem: e.activation(
                rowf[0:1, 0:1], PS[b][0:1, 0:1], AF.Copy, bias=0.0,
                scale=1.0 / nelem), waits=[m], inc=sACT)
            ps_free[b] = a
            v = sch.add("vector", lambda e: e.tensor_scalar_max(
                rowf[0:1, 0:1], rowf[0:1, 0:1], 1e-8), waits=[a], inc=sDVE)
            a = sch.add("scalar", lambda e: e.mul(
                rowf[0:1, 0:1], rowf[0:1, 0:1], 0.5), waits=[v], inc=sACT)
            m = sch.add("tensor", lambda e: e.matmul(
                PS[b][:, 0:1], ones1x128[0:1, :], rowf[0:1, 0:1],
                start=True, stop=True), waits=pw(b, [a]), inc=sPE)
            a = sch.add("scalar", lambda e, ti=ti: e.copy(
                thrs[:, ti:ti + 1], PS[b][:, 0:1]), waits=[m], inc=sACT)
            ps_free[b] = a
            return sch.add("scalar", lambda e, ti=ti: e.mul(
                thrs[:, ti + 1:ti + 2], thrs[:, ti:ti + 1], -1.0), inc=sACT)

        def tern_pass(v_src, v_dst, nblk, width, ti, thr_ready):
            dves, outs = [], []
            ns = 4096 // width
            for i in range(nblk):
                par = i % ns
                w = [dves[i - ns]] if i >= ns else [thr_ready]
                ld = dma(X[:, par * width: par * width + width], v_src(i),
                         semS[par], waits=w)

                def tern(e, p=par, wd=width, ti=ti):
                    xs = X[:, p * wd: p * wd + wd]
                    nc.vector.tensor_scalar(SCR[:, p * wd: p * wd + wd], xs,
                                            thrs[:, ti:ti + 1], None, ALU.is_gt)
                    nc.vector.tensor_scalar(
                        SCR[:, 4096 + p * wd: 4096 + p * wd + wd], xs,
                        thrs[:, ti + 1:ti + 2], None, ALU.is_lt)
                    return nc.vector.tensor_sub(
                        ARENA[:, p * wd: p * wd + wd],
                        SCR[:, p * wd: p * wd + wd],
                        SCR[:, 4096 + p * wd: 4096 + p * wd + wd])
                w2 = [ld] + ([outs[i - ns]] if i >= ns else [])
                v = sch.add("vector", tern, waits=w2, inc=sDVE)
                dves.append(v)
                outs.append(dma(v_dst(i), ARENA[:, par * width: par * width + width],
                                semSCR, waits=[v]))
            return outs[-1]

        def cast_stream(v_src, v_dst, nblk, width):
            dves, outs = [], []
            ns = 4096 // width
            for i in range(nblk):
                par = i % ns
                w = [dves[i - ns]] if i >= ns else [(sDVE, sDVE.v)]
                ld = dma(X[:, par * width: par * width + width], v_src(i),
                         semS[par], waits=w)
                w2 = [ld] + ([outs[i - ns]] if i >= ns else [])
                v = sch.add("vector", lambda e, p=par, wd=width: e.tensor_copy(
                    ARENA[:, p * wd: p * wd + wd],
                    X[:, p * wd: p * wd + wd]), waits=w2, inc=sDVE)
                dves.append(v)
                outs.append(dma(v_dst(i), ARENA[:, par * width: par * width + width],
                                semSCR, waits=[v]))
            return outs[-1]

        a = w_mean_pass(lambda i: wblk(w1_h, i, 8), 32, 1024, 0)
        t1r = scale_math(0, 32, 0, FF * D, a)
        w1s_ready = tern_pass(lambda i: wblk(w1_h, i, 8),
                              lambda i: wblk(w1s, i, 8), 32, 1024, 0, t1r)
        a = w_mean_pass(lambda i: wblk(w2_h, i, 16), 16, 2048, 32)
        t2r = scale_math(32, 16, 2, D * FF, a)
        w2s_ready = tern_pass(lambda i: wblk(w2_h, i, 16),
                              lambda i: wblk(w2s, i, 16), 16, 2048, 2, t2r)
        wins_ready = cast_stream(lambda i: wblk(win_h, i, 8),
                                 lambda i: wblk(wins, i, 8), 32, 1024)
        wouts_ready = cast_stream(lambda i: wblk(wout_h, i, 16),
                                  lambda i: wblk(wouts, i, 16), 8, 2048)
        cds_ready = cast_stream(lambda i: wblk(cdiag_h, i, 8),
                                lambda i: wblk(cds, i, 8), 8, 1024)

        def resident_cast(src_ap, dst_ap, width, rows=128):
            ld = dma(X[0:rows, 0:width], src_ap, semS[0], waits=[(sDVE, sDVE.v)])
            return sch.add("vector", lambda e: e.tensor_copy(
                dst_ap, X[0:rows, 0:width]), waits=[ld], inc=sDVE)

        resident_cast(ddiag_h, ddiag[:, :], 2048)
        resident_cast(
            bass.AP(wxp_h.tensor, 0, [[96, 128], [128 * 96, 16], [1, 96]]),
            wxp[:, :], 16 * 96)
        resident_cast(
            bass.AP(wdt_h.tensor, 0, [[128, 64], [64 * 128, 16], [1, 128]]),
            wdt[:, :], 16 * 128, rows=64)
        ld = dma(X[:, 0:256], alog_h, semS[0], waits=[(sDVE, sDVE.v)])
        a = sch.add("scalar", lambda e: e.activation(
            aneg[:, :], X[:, 0:256], AF.Exp, bias=0.0, scale=1.0),
            waits=[ld], inc=sACT)
        sch.add("vector", lambda e: e.tensor_scalar_mul(
            aneg[:, :], aneg[:, :], -1.0), waits=[a], inc=sDVE)
        setup_v = (sDVE, sDVE.v)
        setup_a = (sACT, sACT.v)

        def r_(i):
            return rowf[0:1, i * TC:(i + 1) * TC]

        def xn_sl(k):
            return SCR[:, ND * TC + k * TC: ND * TC + (k + 1) * TC]

        def u_sl(k):
            return ARENA[:, 8192 + k * TC: 8192 + (k + 1) * TC]

        def emit_stats_d(d, src_ready, src=None, bx=4, bq=5, idx=None):
            idx = d if idx is None else idx
            src = X[:, d * TC:(d + 1) * TC] if src is None else src
            cast = sch.add("vector", lambda e, src=src, d=d: e.tensor_copy(
                SCR[:, d * TC:(d + 1) * TC], src),
                waits=[src_ready] if src_ready is not None else [], inc=sDVE)
            w = pw(bx, [cast, fence(semMISC)]) if idx == 0 else [cast]
            sch.add("tensor", lambda e, d=d, bx=bx, idx=idx: e.matmul(
                PS[bx][0:1, :], onescolb[:, 0:1], SCR[:, d * TC:(d + 1) * TC],
                start=(idx == 0), stop=(idx == ND - 1)), waits=w, inc=sPE)
            par = sq_idx[0] % 2
            sq_idx[0] += 1
            xslc = xsq[:, par * TC:(par + 1) * TC]
            wsq = [src_ready] if src_ready is not None else []
            if sq_war[par] is not None:
                wsq = wsq + [sq_war[par]]
            sq = sch.add("scalar", lambda e, src=src, xslc=xslc: e.activation(
                xslc, src, AF.Square, bias=0.0, scale=1.0),
                waits=wsq, inc=sACT)
            w2 = pw(bq, [sq]) if idx == 0 else [sq]
            mm = sch.add("tensor", lambda e, bq=bq, xslc=xslc, idx=idx: e.matmul(
                PS[bq][0:1, :], onescolb[:, 0:1], xslc,
                start=(idx == 0), stop=(idx == ND - 1)), waits=w2, inc=sPE)
            sq_war[par] = mm
            return mm

        def emit_stats_evac(mm):
            a = sch.add("scalar", lambda e: e.copy(r_(4), PS[6][0:1, :]),
                        waits=[mm], inc=sACT)
            a = sch.add("scalar", lambda e: e.copy(r_(5), PS[7][0:1, :]),
                        inc=sACT)
            ps_free[6] = a
            ps_free[7] = a
            return a

        def emit_ln_rows(mamba, from_rows=None, stats_dep=None):
            if from_rows is None:
                sch.add("scalar", lambda e: e.activation(
                    r_(0), PS[4][0:1, :], AF.Copy, bias=0.0, scale=INV_D),
                    waits=[(sPE, sPE.v)] if stats_dep is None else [stats_dep],
                    inc=sACT)
                a = sch.add("scalar", lambda e: e.activation(
                    r_(1), PS[5][0:1, :], AF.Copy, bias=0.0, scale=INV_D),
                    inc=sACT)
                ps_free[4] = a
                ps_free[5] = a
            else:
                sch.add("scalar", lambda e: e.activation(
                    r_(0), r_(4), AF.Copy, bias=0.0, scale=INV_D),
                    waits=[stats_dep], inc=sACT)
                a = sch.add("scalar", lambda e: e.activation(
                    r_(1), r_(5), AF.Copy, bias=0.0, scale=INV_D), inc=sACT)
            sch.add("scalar", lambda e: e.square(r_(2), r_(0)), inc=sACT)
            v = sch.add("vector", lambda e: e.tensor_sub(r_(1), r_(1), r_(2)),
                        waits=[(sACT, sACT.v)], inc=sDVE)
            a = sch.add("scalar", lambda e: e.activation(
                r_(2), r_(1), AF.Ln, bias=consts[0:1, 0:1], scale=1.0),
                waits=[v], inc=sACT)
            a = sch.add("scalar", lambda e: e.activation(
                r_(3), r_(2), AF.Exp, bias=0.0, scale=-0.5), inc=sACT)
            if mamba:
                v = sch.add("vector", lambda e: e.tensor_scalar_add(
                    r_(2), r_(1), EPS), waits=[a], inc=sDVE)
                v = sch.add("vector", lambda e: e.reciprocal(r_(2), r_(2)),
                            inc=sDVE)
                v = sch.add("vector", lambda e: e.tensor_mul(r_(2), r_(1), r_(2)),
                            inc=sDVE)
                a = sch.add("scalar", lambda e: e.activation(
                    r_(2), r_(2), AF.Ln, bias=consts[0:1, 0:1], scale=1.0),
                    waits=[v], inc=sACT)
                a = sch.add("scalar", lambda e: e.activation(
                    r_(2), r_(2), AF.Exp, bias=0.0, scale=-0.5), inc=sACT)
                v = sch.add("vector", lambda e: e.tensor_mul(r_(3), r_(3), r_(2)),
                            waits=[a], inc=sDVE)
            v = sch.add("vector", lambda e: e.tensor_mul(r_(2), r_(0), r_(3)),
                        waits=[(sACT, sACT.v)], inc=sDVE)
            m1 = sch.add("tensor", lambda e: e.matmul(
                PS[4][:, :], ones1x128[0:1, :], r_(3), start=True, stop=True),
                waits=pw(4, [v]), inc=sPE)
            m2 = sch.add("tensor", lambda e: e.matmul(
                PS[5][:, :], ones1x128[0:1, :], r_(2), start=True, stop=True),
                waits=pw(5, [v]), inc=sPE)
            rs = sch.add("scalar", lambda e: e.copy(repRS[:, :], PS[4][:, :]),
                         waits=[m1, (sDVE, sDVE.v)], inc=sACT)
            ps_free[4] = rs
            mr = sch.add("scalar", lambda e: e.copy(repMR[:, :], PS[5][:, :]),
                         waits=[m2], inc=sACT)
            ps_free[5] = mr
            return rs, mr

        def emit_ln_apply_X(mamba, rs, mr, extra=None):
            last = None
            for d in range(ND):
                sl = slice(d * TC, (d + 1) * TC)
                sch.add("vector", lambda e, sl=sl: e.tensor_mul(
                    SCR[:, sl], X[:, sl], repRS[:, :]),
                    waits=([rs, extra] if d == 0 else []), inc=sDVE)
                last = sch.add("vector", lambda e, d=d, sl=sl: e.tensor_sub(
                    xn_sl(d), SCR[:, sl], repMR[:, :]),
                    waits=[mr] if d == 0 else [], inc=sDVE)
                if mamba:
                    last = sch.add("vector", lambda e, d=d: e.tensor_scalar(
                        xn_sl(d), xn_sl(d), lnwc[:, d:d + 1], lnbc[:, d:d + 1],
                        ALU.mult, ALU.add), inc=sDVE)
            return last

        def wload(view, slot, width, ready_first, sem_idx=None):
            lo = slot * 2048
            w = ([ready_first] if ready_first is not None else []) \
                + wbuf_claim(lo, lo + width)
            f = dma(wbuf[:, lo:lo + width], view, semW[slot if sem_idx is None
                                                       else sem_idx], waits=w)
            return f, lo

        def gen_zh(c, prev_gates_f, xn_f):
            zpend = []
            last_mm = None
            for jp, blk in enumerate(range(8, 16)):
                slot = jp % 4
                ld, lo = wload(wblk(wins, blk, 16), slot, 2048,
                               wins_ready if jp < 4 else None)
                m = None
                for h in range(2):
                    b = 4 + (2 * jp + h) % 4

                    def mmg(e, lo=lo, h=h, b=b):
                        last = None
                        for k in range(8):
                            o = lo + (h * 8 + k) * 128
                            last = nc.tensor.matmul(
                                PS[b][:, :], wbuf[:, o: o + 128], xn_sl(k),
                                start=(k == 0), stop=(k == 7))
                        return last
                    m = sch.add("tensor", mmg, waits=pw(b, [ld, xn_f]),
                                inc=sPE)
                    last_mm = m
                    zpend.append((2 * blk + h - 16, b, m))
                    while len(zpend) > 2:
                        ei2, b2, m2 = zpend.pop(0)
                        a = sch.add("scalar", lambda e, ei=ei2, b=b2: e.copy(
                            ARENA[:, ei * TC:(ei + 1) * TC], PS[b][:, :]),
                            waits=[m2], inc=sACT)
                        ps_free[b2] = a
                wbuf_use(lo, lo + 2048, m)
                zh_state["mm"] = last_mm
                yield
            for ei2, b2, m2 in zpend:
                a = sch.add("scalar", lambda e, ei=ei2, b=b2: e.copy(
                    ARENA[:, ei * TC:(ei + 1) * TC], PS[b][:, :]),
                    waits=[m2], inc=sACT)
                ps_free[b2] = a
            w = [(sACT, sACT.v)]
            if prev_gates_f is not None:
                w.append(prev_gates_f)
            dma(bass.AP(z_d.tensor, 0, [[NE * TC, 128], [1, NE * TC]]),
                ARENA[:, 0:NE * TC], semZS, waits=w)
            zh_state["mm"] = last_mm

        def gen_ffn_A(c, x_ready, wout_pref=None, mstate=None):
            rs, mr = emit_ln_rows(False, stats_dep=None)
            yield
            xn = emit_ln_apply_X(False, rs, mr, extra=fence(semSP))
            yield
            gelu_last = None
            gpend = []
            for fp in range(NF // 2):
                slot = fp % 4
                ld, lo = wload(wblk(w1s, fp, 16), slot, 2048,
                               w1s_ready if fp < 4 else None)
                m = None
                for h in range(2):
                    f = 2 * fp + h
                    b = 4 + (f % 4)

                    def mmg(e, lo=lo, h=h, b=b):
                        last = None
                        for k in range(8):
                            o = lo + (h * 8 + k) * 128
                            last = nc.tensor.matmul(
                                PS[b][:, :], wbuf[:, o: o + 128], xn_sl(k),
                                start=(k == 0), stop=(k == 7))
                        return last
                    m = sch.add("tensor", mmg,
                                waits=pw(b, [ld, xn]), inc=sPE)
                    gpend.append((f, b, m))
                    while len(gpend) > 3:
                        f2, b2, m2 = gpend.pop(0)
                        gelu_last = sch.add(
                            "scalar", lambda e, f=f2, b=b2: e.activation(
                                ARENA[:, f * TC:(f + 1) * TC], PS[b][:, :],
                                AF.Gelu, bias=b1c[:, f:f + 1], scale=1.0),
                            waits=[m2], inc=sACT)
                        ps_free[b2] = gelu_last
                wbuf_use(lo, lo + 2048, m)
                if fp % 2 == 1:
                    yield
            for f2, b2, m2 in gpend:
                gelu_last = sch.add("scalar", lambda e, f=f2, b=b2: e.activation(
                    ARENA[:, f * TC:(f + 1) * TC], PS[b][:, :], AF.Gelu,
                    bias=b1c[:, f:f + 1], scale=1.0), waits=[m2], inc=sACT)
                ps_free[b2] = gelu_last
            gpend = []
            yield
            pend = None

            spend = []

            def do_tail(d, a):
                par = d % 2
                spsl = sp[:, par * TC: par * TC + TC]
                res = sch.add("vector",
                              lambda e, spsl=spsl: e.scalar_tensor_tensor(
                                  X[:, d * TC:(d + 1) * TC], spsl, 0.5,
                                  X[:, d * TC:(d + 1) * TC],
                                  ALU.mult, ALU.add), waits=[a], inc=sDVE)
                sp_war[par] = res
                spend.append((d, res))
                if len(spend) > 1:
                    emit_stats_d(*spend.pop(0))
                gen_ffn_A.res = res
                if d == ND - 1:
                    while spend:
                        emit_stats_d(*spend.pop(0))

            mm_q = []

            def do_evac(d, b, m):
                par = d % 2
                spsl = sp[:, par * TC: par * TC + TC]
                we = [m] + ([sp_war[par]] if sp_war[par] is not None else [])
                a = sch.add("scalar", lambda e, d=d, b=b, spsl=spsl: e.activation(
                    spsl, PS[b][:, :], AF.Identity, bias=b2c[:, d:d + 1],
                    scale=1.0), waits=we, inc=sACT)
                ps_free[b] = a
                return (d, a)

            for d in range(ND):
                ld0, lo0 = wload(
                    bass.AP(w2s.tensor, (d * 32) * 16384,
                            [[128, 128], [16384, 16], [1, 128]]),
                    (2 * d) % 4, 2048, w2s_ready if d < 2 else None)
                ld1, lo1 = wload(
                    bass.AP(w2s.tensor, (d * 32 + 16) * 16384,
                            [[128, 128], [16384, 16], [1, 128]]),
                    (2 * d + 1) % 4, 2048, w2s_ready if d < 2 else None)
                b = 6 + (d % 2)

                if len(mm_q) > 1:
                    if pend is not None:
                        do_tail(*pend)
                    pend = do_evac(*mm_q.pop(0))

                def mmg2(e, lo0=lo0, lo1=lo1, b=b):
                    last = None
                    for k in range(32):
                        lo = lo0 if k < 16 else lo1
                        o = lo + (k % 16) * 128
                        last = nc.tensor.matmul(
                            PS[b][:, :], wbuf[:, o: o + 128],
                            ARENA[:, k * TC:(k + 1) * TC],
                            start=(k == 0), stop=(k == 31))
                    return last
                m = sch.add("tensor", mmg2, waits=pw(b, [ld0, ld1, gelu_last]),
                            inc=sPE)
                wbuf_use(lo0, lo0 + 2048, m)
                wbuf_use(lo1, lo1 + 2048, m)
                mm_q.append((d, b, m))
                yield
            while mm_q:
                if pend is not None:
                    do_tail(*pend)
                pend = do_evac(*mm_q.pop(0))
            do_tail(*pend)
            if wout_pref is not None and "done" not in wout_pref:
                for d2 in range(2):
                    wout_pref[d2] = wload(wblk(wouts, d2, 16), d2 % 2, 2048,
                                          wouts_ready)
                if mstate is not None:
                    for d2 in range(2):
                        par = d2 % 2
                        w = [mstate["sp_xn"]]
                        if xnsl_war[par] is not None:
                            w.append(xnsl_war[par])
                        wout_pref[("ldn", d2)] = dma(
                            xnsl[:, par * TC:(par + 1) * TC],
                            bass.AP(xn_d.tensor, d2 * TC,
                                    [[ND * TC, 128], [1, TC]]),
                            semNL[par], waits=w)
                        xnsl_war[par] = None
                        w = [mstate["sp_x1"]]
                        if xsl_war[par] is not None:
                            w.append(xsl_war[par])
                        wout_pref[("lds", d2)] = dma(
                            xsl[:, par * TC:(par + 1) * TC],
                            bass.AP(x1_d.tensor, d2 * TC,
                                    [[ND * TC, 128], [1, TC]]),
                            semSL[par], waits=w)
                        xsl_war[par] = None

        def gen_ffn_C(c, stats_dep, pre_lds=None, zh=None):
            rs, mr = emit_ln_rows(False, from_rows=True, stats_dep=stats_dep)
            yield
            lds = dict(pre_lds) if pre_lds else {}
            for d in range(2):
                if d in lds:
                    continue
                w = [rs] if d == 0 else []
                if xsl_war[d % 2] is not None:
                    w.append(xsl_war[d % 2])
                lds[d] = dma(xsl[:, (d % 2) * TC:(d % 2 + 1) * TC],
                             bass.AP(x2_d.tensor, d * TC,
                                     [[ND * TC, 128], [1, TC]]),
                             semSL[d % 2], waits=w)
            xn = None
            for d in range(ND):
                par = d % 2
                wmul = [lds[d]] + ([rs, fence(semSP)] if d == 0 else [])
                if d == 0 and zh is not None and zh["mm"] is not None:
                    wmul.append(zh["mm"])
                sch.add("vector", lambda e, d=d, par=par: e.tensor_mul(
                    SCR[:, d * TC:(d + 1) * TC],
                    xsl[:, par * TC:(par + 1) * TC], repRS[:, :]),
                    waits=wmul, inc=sDVE)
                xn = sch.add("vector", lambda e, d=d: e.tensor_sub(
                    xn_sl(d), SCR[:, d * TC:(d + 1) * TC], repMR[:, :]),
                    waits=[mr] if d == 0 else [], inc=sDVE)
                xsl_war[par] = xn
                if d + 2 < ND:
                    lds[d + 2] = dma(xsl[:, par * TC:(par + 1) * TC],
                                     bass.AP(x2_d.tensor, (d + 2) * TC,
                                             [[ND * TC, 128], [1, TC]]),
                                     semSL[par], waits=[xn])
                if d % 2 == 1:
                    yield
            gelu_last = None
            gpend = []
            zspill_w = [fence(semZS)] if zh is not None else []
            for fp in range(NF // 2):
                slot = fp % 4
                ld, lo = wload(wblk(w1s, fp, 16), slot, 2048, None)
                m = None
                for h in range(2):
                    f = 2 * fp + h
                    b = 4 + (f % 4)

                    def mmg(e, lo=lo, h=h, b=b):
                        last = None
                        for k in range(8):
                            o = lo + (h * 8 + k) * 128
                            last = nc.tensor.matmul(
                                PS[b][:, :], wbuf[:, o: o + 128], xn_sl(k),
                                start=(k == 0), stop=(k == 7))
                        return last
                    m = sch.add("tensor", mmg, waits=pw(b, [ld, xn]), inc=sPE)
                    gpend.append((f, b, m))
                    while len(gpend) > 3:
                        f2, b2, m2 = gpend.pop(0)
                        gelu_last = sch.add(
                            "scalar", lambda e, f=f2, b=b2: e.activation(
                                ARENA[:, f * TC:(f + 1) * TC], PS[b][:, :],
                                AF.Gelu, bias=b1c[:, f:f + 1], scale=1.0),
                            waits=[m2] + zspill_w, inc=sACT)
                        zspill_w = []
                        ps_free[b2] = gelu_last
                wbuf_use(lo, lo + 2048, m)
                if fp % 2 == 1:
                    yield
            for f2, b2, m2 in gpend:
                gelu_last = sch.add("scalar", lambda e, f=f2, b=b2: e.activation(
                    ARENA[:, f * TC:(f + 1) * TC], PS[b][:, :], AF.Gelu,
                    bias=b1c[:, f:f + 1], scale=1.0), waits=[m2], inc=sACT)
                ps_free[b2] = gelu_last
            gpend = []
            yield
            pend = None

            def do_tail_c(d, a, lds2):
                par = d % 2
                spsl = sp[:, par * TC: par * TC + TC]
                res = sch.add("vector",
                              lambda e, par=par, spsl=spsl: e.scalar_tensor_tensor(
                                  xsl[:, par * TC:(par + 1) * TC], spsl, 0.5,
                                  xsl[:, par * TC:(par + 1) * TC],
                                  ALU.mult, ALU.add),
                              waits=[a, lds2], inc=sDVE)
                sp_war[par] = res
                xsl_war[par] = dma(
                    bass.AP(out_h.tensor, c * TC + d * 128 * L,
                            [[L, 128], [1, TC]]),
                    xsl[:, par * TC:(par + 1) * TC], semO, waits=[res])

            mm_q = []

            def do_evac_c(d, b, m):
                par = d % 2
                w = []
                if xsl_war[par] is not None:
                    w.append(xsl_war[par])
                lds2 = dma(xsl[:, par * TC:(par + 1) * TC],
                           bass.AP(x2_d.tensor, d * TC,
                                   [[ND * TC, 128], [1, TC]]),
                           semSL[par], waits=w)
                xsl_war[par] = None
                spsl = sp[:, par * TC: par * TC + TC]
                we = [m] + ([sp_war[par]] if sp_war[par] is not None else [])
                a = sch.add("scalar", lambda e, d=d, b=b, spsl=spsl: e.activation(
                    spsl, PS[b][:, :], AF.Identity, bias=b2c[:, d:d + 1],
                    scale=1.0), waits=we, inc=sACT)
                ps_free[b] = a
                return (d, a, lds2)

            for d in range(ND):
                ld0, lo0 = wload(
                    bass.AP(w2s.tensor, (d * 32) * 16384,
                            [[128, 128], [16384, 16], [1, 128]]),
                    (2 * d) % 4, 2048, None)
                ld1, lo1 = wload(
                    bass.AP(w2s.tensor, (d * 32 + 16) * 16384,
                            [[128, 128], [16384, 16], [1, 128]]),
                    (2 * d + 1) % 4, 2048, None)
                b = 6 + (d % 2)

                if len(mm_q) > 1:
                    if pend is not None:
                        do_tail_c(*pend)
                    pend = do_evac_c(*mm_q.pop(0))

                def mmg2(e, lo0=lo0, lo1=lo1, b=b):
                    last = None
                    for k in range(32):
                        lo = lo0 if k < 16 else lo1
                        o = lo + (k % 16) * 128
                        last = nc.tensor.matmul(
                            PS[b][:, :], wbuf[:, o: o + 128],
                            ARENA[:, k * TC:(k + 1) * TC],
                            start=(k == 0), stop=(k == 31))
                    return last
                m = sch.add("tensor", mmg2, waits=pw(b, [ld0, ld1, gelu_last]),
                            inc=sPE)
                wbuf_use(lo0, lo0 + 2048, m)
                wbuf_use(lo1, lo1 + 2048, m)
                mm_q.append((d, b, m))
                yield
            while mm_q:
                if pend is not None:
                    do_tail_c(*pend)
                pend = do_evac_c(*mm_q.pop(0))
            do_tail_c(*pend)

        def emit_m1(c, x_ready):
            rs, mr = emit_ln_rows(True, stats_dep=None)
            xn = emit_ln_apply_X(True, rs, mr)
            pair_order = list(range(8))
            for jp, blk in enumerate(pair_order):
                slot = jp % 4
                ld, lo = wload(wblk(wins, blk, 16), slot, 2048,
                               wins_ready if jp < 4 else None)
                m = None
                for h in range(2):
                    f = 2 * blk + h
                    b = (2 * jp + h) % 4
                    def mmg(e, lo=lo, h=h, b=b):
                        last = None
                        for k in range(8):
                            o = lo + (h * 8 + k) * 128
                            last = nc.tensor.matmul(
                                PS[b][:, :], wbuf[:, o: o + 128], xn_sl(k),
                                start=(k == 0), stop=(k == 7))
                        return last
                    m = sch.add("tensor", mmg, waits=pw(b, [ld, xn]), inc=sPE)
                    if f >= 16:
                        ei = f - 16
                        a = sch.add("scalar", lambda e, ei=ei, b=b: e.copy(
                            ARENA[:, ei * TC:(ei + 1) * TC], PS[b][:, :]),
                            waits=[m], inc=sACT)
                    else:
                        ei = f
                        a = sch.add("scalar", lambda e, ei=ei, b=b: e.copy(
                            XSYG[:, ei * (TC + 3) + 3: (ei + 1) * (TC + 3)],
                            PS[b][:, :]), waits=[m], inc=sACT)
                    ps_free[b] = a
                wbuf_use(lo, lo + 2048, m)
            sp_z = (semZS, 16 * (c + 1))
            sp_xn = dma(bass.AP(xn_d.tensor, 0, [[ND * TC, 128], [1, ND * TC]]),
                        SCR[:, ND * TC: 2 * ND * TC], semSP,
                        waits=[xn, fence(semNL[0]), fence(semNL[1])])
            sp_x1 = dma(bass.AP(x1_d.tensor, 0, [[ND * TC, 128], [1, ND * TC]]),
                        X[:, :], semSP,
                        waits=[(sDVE, sDVE.v), fence(semSL[0]), fence(semSL[1])])
            hr = None
            for ei in range(NE):
                hr = sch.add("scalar", lambda e, ei=ei: e.copy(
                    XSYG[:, ei * (TC + 3): ei * (TC + 3) + 3],
                    halos[:, ei * 3:(ei + 1) * 3]), inc=sACT)
            xs_ready = hr
            u_last = None
            for ei in range(NE):
                cp = ei % 2
                wcd = [cds_ready] if ei < 2 else []
                if cdsl_war[cp] is not None:
                    wcd.append(cdsl_war[cp])
                cld = dma(cdsl[:, cp * 512:(cp + 1) * 512],
                          bass.AP(cds.tensor, ei * 4 * 16384,
                                  [[128, 128], [16384, 4], [1, 128]]),
                          semC[cp], waits=wcd)
                b = 2 + ei % 2
                def convg(e, ei=ei, b=b, cp=cp):
                    last = None
                    for tap in range(4):
                        last = nc.tensor.matmul(
                            PS[b][:, :],
                            cdsl[:, cp * 512 + tap * 128: cp * 512 + (tap + 1) * 128],
                            XSYG[:, ei * (TC + 3) + tap: ei * (TC + 3) + tap + TC],
                            start=(tap == 0), stop=(tap == 3))
                    return last
                m = sch.add("tensor", convg, waits=pw(b, [xs_ready, cld]), inc=sPE)
                cdsl_war[cp] = m
                a = sch.add("scalar", lambda e, ei=ei, b=b: e.activation(
                    u_sl(ei), PS[b][:, :], AF.Silu, bias=cbc[:, ei:ei + 1],
                    scale=1.0), waits=[m], inc=sACT)
                ps_free[b] = a
                u_last = sch.add("scalar", lambda e, ei=ei: e.copy(
                    halos[:, ei * 3:(ei + 1) * 3],
                    XSYG[:, ei * (TC + 3) + TC: ei * (TC + 3) + TC + 3]),
                    inc=sACT)
            u_ready = u_last

            def xprojg(e):
                last = None
                for k in range(NE):
                    last = nc.tensor.matmul(
                        PS[5][0:64, :], wxp[:, k * 96: k * 96 + 64], u_sl(k),
                        start=(k == 0), stop=(k == NE - 1))
                for k in range(NE):
                    last = nc.tensor.matmul(
                        PS[4][0:32, :], wxp[:, k * 96 + 64: k * 96 + 96], u_sl(k),
                        start=(k == 0), stop=(k == NE - 1))
                return last
            m = sch.add("tensor", xprojg, waits=pw(5, pw(4, [u_ready])), inc=sPE)
            a1 = sch.add("scalar", lambda e: e.copy(dtrbf[:, :], PS[5][0:64, :]),
                         waits=[m], inc=sACT)
            ps_free[5] = a1
            wbc = [m]
            if bcw_prev[0] is not None:
                wbc.append(bcw_prev[0])
            a2 = sch.add("scalar", lambda e: e.copy(bcbf[:, :], PS[4][0:32, :]),
                         waits=wbc, inc=sACT)
            ps_free[4] = a2
            dma(bass.AP(bc_dram.tensor, 0, [[2 * TC, 16], [1, TC]]),
                bcbf[0:16, :], semSP, waits=[a2])
            bcw = dma(bass.AP(bc_dram.tensor, TC, [[2 * TC, 16], [1, TC]]),
                      bcbf[16:32, :], semSP)
            bcw_prev[0] = bcw
            dt_dones, dtu_dones = {}, {}
            for ei in range(NE):
                b = 2 + ei % 2
                m = sch.add("tensor", lambda e, ei=ei, b=b: e.matmul(
                    PS[b][:, :], wdt[:, ei * 128:(ei + 1) * 128], dtrbf[:, :],
                    start=True, stop=True), waits=pw(b, [a1]), inc=sPE)
                par = ei % 2
                spsl = sp[:, par * TC: par * TC + TC]
                we = [m] + ([sp_war[par]] if sp_war[par] is not None else [])
                a = sch.add("scalar", lambda e, spsl=spsl, ei=ei, b=b: e.activation(
                    spsl, PS[b][:, :], AF.Exp, bias=bdtc[:, ei:ei + 1],
                    scale=1.0), waits=we, inc=sACT)
                ps_free[b] = a
                a = sch.add("scalar", lambda e, spsl=spsl, ei=ei: e.activation(
                    dtbf[:, ei * TC:(ei + 1) * TC], spsl, AF.Ln,
                    bias=consts[:, 1:2], scale=1.0), inc=sACT)
                sp_war[par] = a
                dt_dones[ei] = a
                dtu_dones[ei] = sch.add("vector", lambda e, ei=ei: e.tensor_mul(
                    dtu[:, ei * TC:(ei + 1) * TC], dtbf[:, ei * TC:(ei + 1) * TC],
                    u_sl(ei)), waits=[a], inc=sDVE)
            sp_u = dma(bass.AP(u_d.tensor, 0, [[NE * TC, 128], [1, NE * TC]]),
                       ARENA[:, 8192:16384], semSP,
                       waits=[(sPE, sPE.v), (sDVE, sDVE.v)])
            return dict(dt=dt_dones, dtu=dtu_dones, bcw=bcw, u_ready=u_ready,
                        sp_x1=sp_x1, sp_xn=sp_xn, sp_z=sp_z, sp_u=sp_u,
                        xn_f=xn)

        def emit_scan(c, m1, pumps):
            LA = 2
            bc_free = {}
            bc_ready = {}
            gate_done = {}
            dA_done = {}
            dA_war = {}
            h_war = {}
            tmp_war = {0: None, 1: None}
            last_idacc = {}
            dt_dones = m1["dt"]
            dtu_dones = m1["dtu"]
            bcw = m1["bcw"]
            u_ready = m1["u_ready"]

            def bcast_pair(np5, half, cnt):
                return bass.AP(BC, (2 * np5 + half) * TC,
                               [[10 * TC, 128], [0, cnt], [1, TC]])

            def pump():
                for g in pumps:
                    try:
                        next(g)
                        return
                    except StopIteration:
                        continue

            tg0 = 0
            zload_done = {}
            for gi, g in enumerate(GROUPS):
                ng = len(g)
                for bi, e_ in enumerate(g):
                    bank = bi
                    cp = e_ % 2
                    wu = [m1["sp_u"]]
                    if cdsl_war[cp] is not None:
                        wu.append(cdsl_war[cp])
                    ul = dma(cdsl[:, cp * 512:(cp + 1) * 512],
                             bass.AP(u_d.tensor, e_ * TC,
                                     [[NE * TC, 128], [1, TC]]),
                             semC[cp], waits=wu)
                    mu_ = sch.add("tensor", lambda e, e_=e_, bank=bank, cp=cp: e.matmul(
                        PS[bank][:, :], ddiag[:, e_ * 128:(e_ + 1) * 128],
                        cdsl[:, cp * 512:(cp + 1) * 512],
                        start=True, stop=False, skip_group_check=True),
                        waits=pw(bank, [ul]), inc=sPE)
                    cdsl_war[cp] = mu_
                zls = {}

                def emit_zls(g=g):
                    for bi, e_ in enumerate(g[:2]):
                        par = e_ % 2
                        wz = [m1["sp_z"]]
                        if cdsl_war[par] is not None:
                            wz.append(cdsl_war[par])
                        zls[e_] = dma(cdsl[:, par * 512:(par + 1) * 512],
                                      bass.AP(z_d.tensor, e_ * TC,
                                              [[NE * TC, 128], [1, TC]]),
                                      semZL[par], waits=wz)
                        cdsl_war[par] = None
                if gi > 0:
                    emit_zls()
                seq = [(n, i) for n in range(NST) for i in range(ng)]
                for si in range(len(seq) + LA):
                    if gi == 0 and si == 12 * ng:
                        emit_zls()
                    if si < len(seq):
                        n, i = seq[si]
                        if i == 0:
                            np5 = n % 5
                            wb = [bcw]
                            if np5 in bc_free:
                                wb.append(bc_free[np5])
                            bc_ready[n] = dma(
                                BC[:, 2 * np5 * TC: (2 * np5 + 2) * TC],
                                b_bcpair(n), semB[np5], waits=wb)
                        tg = tg0 + si
                        tp = tg % 3
                        wa = [dt_dones[g[i]]]
                        if tp in dA_war:
                            wa.append(dA_war[tp])
                        dA_done[tg] = sch.add(
                            "scalar", lambda e, e_=g[i], n=n, tp=tp: e.activation(
                                dA2[:, tp * TC:(tp + 1) * TC],
                                dtbf[:, e_ * TC:(e_ + 1) * TC], AF.Exp,
                                bias=0.0,
                                scale=aneg[:, e_ * NST + n: e_ * NST + n + 1]),
                            waits=wa, inc=sACT)
                    if si >= LA:
                        n, i = seq[si - LA]
                        tg = tg0 + si - LA
                        tp = tg % 3
                        e_ = g[i]
                        idx = e_ * NST + n
                        np5 = n % 5
                        if i == 0:
                            gb = g[0] * TC
                            sch.add("vector",
                                    lambda e, gb=gb, np5=np5, ng=ng: e.tensor_tensor(
                                        dBu[:, 0:ng * TC], dtu[:, gb: gb + ng * TC],
                                        bcast_pair(np5, 0, ng), ALU.mult),
                                    waits=[bc_ready[n], dtu_dones[g[ng - 1]]],
                                    inc=sDVE)
                        ws = [dA_done[tg]]
                        if i in h_war:
                            ws.append(h_war[i])
                        sc = sch.add(
                            "vector", lambda e, tp=tp, i=i, idx=idx: e.tensor_tensor_scan(
                                hbuf[:, i * TC:(i + 1) * TC],
                                dA2[:, tp * TC:(tp + 1) * TC],
                                dBu[:, i * TC:(i + 1) * TC],
                                hstate[:, idx:idx + 1],
                                ALU.mult, ALU.add), waits=ws, inc=sDVE)
                        dA_war[tp] = sc
                        cy = sch.add("scalar", lambda e, i=i, idx=idx: e.copy(
                            hstate[:, idx:idx + 1],
                            hbuf[:, i * TC + TC - 1: i * TC + TC]),
                            waits=[sc], inc=sACT)
                        h_war[i] = cy
                        if i == ng - 1:
                            rp = n % 2
                            ybuf = tmpb if rp == 0 else XSYG
                            yb = 0 if rp == 0 else 12 * TC
                            wy = []
                            if tmp_war[rp] is not None:
                                wy.append(tmp_war[rp])
                            ym = sch.add(
                                "vector",
                                lambda e, np5=np5, ng=ng, ybuf=ybuf, yb=yb:
                                e.tensor_tensor(
                                    ybuf[:, yb:yb + ng * TC], hbuf[:, 0:ng * TC],
                                    bcast_pair(np5, 1, ng), ALU.mult),
                                waits=wy, inc=sDVE)
                            bc_free[np5] = ym
                            mmi = None
                            for bi2, e2 in enumerate(g):
                                mmi = sch.add(
                                    "tensor",
                                    lambda e, bi2=bi2, n=n, ybuf=ybuf, yb=yb:
                                    e.matmul(
                                        PS[bi2][:, :], eye[:, :],
                                        ybuf[:, yb + bi2 * TC:yb + (bi2 + 1) * TC],
                                        start=False, stop=(n == NST - 1),
                                        skip_group_check=True),
                                    waits=[ym] if bi2 == 0 else [], inc=sPE)
                                last_idacc[e2] = mmi
                            tmp_war[rp] = mmi
                            pump()
                tg0 += len(seq)
                for bi, e_ in enumerate(g):
                    par = e_ % 2
                    if e_ in zls:
                        zl = zls[e_]
                    else:
                        wz = [m1["sp_z"]]
                        if cdsl_war[par] is not None:
                            wz.append(cdsl_war[par])
                        zl = dma(cdsl[:, par * 512:(par + 1) * 512],
                                 bass.AP(z_d.tensor, e_ * TC,
                                         [[NE * TC, 128], [1, TC]]),
                                 semZL[par], waits=wz)
                    wsil = [zl]
                    if (e_ - 2) in gate_done:
                        wsil.append(gate_done[e_ - 2])
                    az = sch.add("scalar", lambda e, par=par: e.activation(
                        zq[:, par * TC:(par + 1) * TC],
                        cdsl[:, par * 512:(par + 1) * 512], AF.Silu,
                        bias=0.0, scale=1.0), waits=wsil, inc=sACT)
                    gv = sch.add(
                        "vector", lambda e, e_=e_, bi=bi, par=par: e.tensor_mul(
                            XSYG[:, e_ * TC:(e_ + 1) * TC], PS[bi][:, :],
                            zq[:, par * TC:(par + 1) * TC]),
                        waits=[az, last_idacc[e_]], inc=sDVE)
                    gate_done[e_] = gv
                    cdsl_war[par] = gv
                    ps_free[bi] = gv
            return (sDVE, sDVE.v)

        def emit_m2(c, m1, gates, pref=None):
            for pos, d in enumerate([0, 1, 4, 5, 2, 3, 6, 7]):
                slot = d % 4
                if pref is not None and d in pref:
                    ld, lo = pref[d]
                else:
                    ld, lo = wload(wblk(wouts, d, 16), slot, 2048,
                                   wouts_ready if d < 4 else None)
                lane = d % 4
                if lane < 2:
                    xnv, xnw = xnsl[:, lane * TC:(lane + 1) * TC], xnsl_war
                    x1v, x1w = xsl[:, lane * TC:(lane + 1) * TC], xsl_war
                    lk = lane
                else:
                    lk = lane - 2
                    xnv, xnw = cdsl[:, lk * 512:(lk + 1) * 512], cdsl_war
                    x1v, x1w = sp[:, lk * TC:(lk + 1) * TC], sp_war
                if pref is not None and ("ldn", d) in pref:
                    ldn = pref[("ldn", d)]
                else:
                    ldw = [m1["sp_xn"]]
                    if lane >= 2 and d < 4:
                        ldw.append(gates)
                    if xnw[lk] is not None:
                        ldw.append(xnw[lk])
                    ldn = dma(xnv, bass.AP(xn_d.tensor, d * TC,
                                           [[ND * TC, 128], [1, TC]]),
                              semNL[lane % 2], waits=ldw)
                if pref is not None and ("lds", d) in pref:
                    lds = pref[("lds", d)]
                else:
                    ldw = [m1["sp_x1"]]
                    if x1w[lk] is not None:
                        ldw.append(x1w[lk])
                    lds = dma(x1v, bass.AP(x1_d.tensor, d * TC,
                                           [[ND * TC, 128], [1, TC]]),
                              semSL[lane % 2], waits=ldw)
                b = d % 2
                def mmg3(e, lo=lo, b=b, xnv=xnv):
                    for k in range(NE):
                        nc.tensor.matmul(
                            PS[b][:, :], wbuf[:, lo + k * 128: lo + (k + 1) * 128],
                            XSYG[:, k * TC:(k + 1) * TC],
                            start=(k == 0), stop=False)
                    return nc.tensor.matmul(
                        PS[b][:, :], eye[:, :], xnv,
                        start=False, stop=True)
                m = sch.add("tensor", mmg3, waits=pw(b, [ld, ldn, gates]), inc=sPE)
                wbuf_use(lo, lo + 2048, m)
                xnw[lk] = m
                res = sch.add("vector", lambda e, x1v=x1v, b=b: e.tensor_add(
                    x1v, x1v, PS[b][:, :]),
                    waits=[m, lds], inc=sDVE)
                ps_free[b] = res
                smm = emit_stats_d(d, res, src=x1v, bx=6, bq=7, idx=pos)
                st = dma(bass.AP(x2_d.tensor, d * TC,
                                 [[ND * TC, 128], [1, TC]]),
                         x1v, semSP, waits=[res])
                x1w[lk] = st
            if pref is not None:
                pref["done"] = True
            clds = {}
            for d in range(2):
                par = d % 2
                w = []
                if xsl_war[par] is not None:
                    w.append(xsl_war[par])
                clds[d] = dma(xsl[:, par * TC:(par + 1) * TC],
                              bass.AP(x2_d.tensor, d * TC,
                                      [[ND * TC, 128], [1, TC]]),
                              semSL[par], waits=w)
                xsl_war[par] = None
            return emit_stats_evac(smm), clds

        prev_gates = None
        m1_state = None
        wp_prev = None
        clds_prev = None
        stats_x2_dep = None
        genA_next = None
        xin_prev = None
        for c in range(NCH):
            if c == 0:
                w = [setup_a, setup_v]
                xin = dma(X[:, :], _xio(x_h, 0), semX, waits=w)
                for d in range(ND):
                    emit_stats_d(d, xin)
                gA = gen_ffn_A(0, xin)
                for _ in gA:
                    pass
                r1 = gen_ffn_A.res
            else:
                stats_x2_dep, clds_prev = emit_m2(c - 1, m1_state, prev_gates,
                                                  pref=wp_prev)
                wp_prev = None
                for _ in genA_next:
                    pass
                r1 = gen_ffn_A.res
            m1_state = emit_m1(c, r1)
            zh_gen = gen_zh(c, prev_gates, m1_state["xn_f"])
            pumps = [zh_gen]
            genC_cur = None
            if c > 0:
                genC_cur = gen_ffn_C(c - 1, stats_x2_dep, pre_lds=clds_prev,
                                     zh=zh_state)
                pumps.append(genC_cur)
            if c + 1 < NCH:
                xin = dma(X[:, :], _xio(x_h, c + 1), semX,
                          waits=[m1_state["sp_x1"], (sACT, sACT.v)])
                wp_cur = {}

                def genA_wrap(c=c, xin=xin, wp=wp_cur, ms=m1_state):
                    for d in range(ND):
                        emit_stats_d(d, xin)
                    yield
                    yield from gen_ffn_A(c + 1, xin, wout_pref=wp, mstate=ms)
                genA_next = genA_wrap()
                wp_prev = wp_cur
                import os as _os
                if _os.environ.get("NO_APUMP") != "1":
                    pumps.append(genA_next)
            prev_gates = emit_scan(c, m1_state, pumps)
            for _ in zh_gen:
                pass
            if genC_cur is not None:
                for _ in genC_cur:
                    pass
        stats_x2_dep, clds_prev = emit_m2(NCH - 1, m1_state, prev_gates,
                                          pref=wp_prev)
        for _ in gen_ffn_C(NCH - 1, stats_x2_dep, pre_lds=clds_prev,
                           zh=zh_state):
            pass
        sch.add("sync", lambda e: e.nop(), waits=[fence(semO)])

        with nc.Block() as block:
            sch.emit(block)

    return nc


def _prep_shared(conv_w, conv_b, w_xproj, w_dt, b_dt, a_log, d_skip, w_out,
                 ln_w, ln_b, W1, b1, W2, b2, w_in):
    bf = ml_dtypes.bfloat16
    d = {}
    d["w1blk"] = np.ascontiguousarray(
        W1.T.reshape(8, 128, NF, 128).transpose(2, 0, 1, 3))
    d["w2blk"] = np.ascontiguousarray(
        W2.T.reshape(32, 128, ND, 128).transpose(2, 0, 1, 3))
    d["winblk"] = np.ascontiguousarray(
        w_in.T.reshape(8, 128, NF, 128).transpose(2, 0, 1, 3))
    d["woutblk"] = np.ascontiguousarray(
        w_out.T.reshape(16, 128, ND, 128).transpose(2, 0, 1, 3))
    d["wxpblk"] = np.ascontiguousarray(w_xproj.T.reshape(16, 128, 96))
    d["wdtblk"] = np.ascontiguousarray(
        w_dt.T.reshape(64, 16, 128).transpose(1, 0, 2))
    idx = np.arange(128)
    cd = np.zeros((NE, 4, 128, 128), np.float32)
    for e_ in range(NE):
        for tap in range(4):
            cd[e_, tap, idx, idx] = conv_w[e_ * 128:(e_ + 1) * 128, tap]
    d["cdiag"] = cd
    dd = np.zeros((128, NE * 128), np.float32)
    for e_ in range(NE):
        dd[idx, e_ * 128 + idx] = d_skip[e_ * 128:(e_ + 1) * 128]
    d["ddiag"] = dd
    d["alogc"] = np.ascontiguousarray(
        a_log.reshape(NE, 128, NST).transpose(1, 0, 2).reshape(128, NE * NST))
    d["b1c"] = np.ascontiguousarray(b1.reshape(NF, 128).T)
    d["b2c"] = np.ascontiguousarray(b2.reshape(ND, 128).T)
    d["cbc"] = np.ascontiguousarray(conv_b.reshape(NE, 128).T)
    d["bdtc"] = np.ascontiguousarray(b_dt.reshape(NE, 128).T)
    d["lnwc"] = np.ascontiguousarray(ln_w.reshape(ND, 128).T)
    d["lnbc"] = np.ascontiguousarray(ln_b.reshape(ND, 128).T)
    d["eye128"] = np.eye(128, dtype=bf)
    d["onescol"] = np.ones((128, 1), np.float32)
    d["onescolb"] = np.ones((128, 1), dtype=bf)
    return d


def kernel(L0_attn_x, L0_attn_ln_w, L0_attn_ln_b, L0_attn_w_in, L0_attn_conv_w,
           L0_attn_conv_b, L0_attn_w_xproj, L0_attn_w_dt, L0_attn_b_dt,
           L0_attn_a_log, L0_attn_d_skip, L0_attn_w_out, L0_ffn_W1, L0_ffn_b1,
           L0_ffn_W2, L0_ffn_b2):
    x = np.asarray(L0_attn_x, np.float32)
    B, L, _ = x.shape
    nc = build_nc(L)
    shared = _prep_shared(
        np.asarray(L0_attn_conv_w, np.float32),
        np.asarray(L0_attn_conv_b, np.float32),
        np.asarray(L0_attn_w_xproj, np.float32),
        np.asarray(L0_attn_w_dt, np.float32),
        np.asarray(L0_attn_b_dt, np.float32),
        np.asarray(L0_attn_a_log, np.float32),
        np.asarray(L0_attn_d_skip, np.float32),
        np.asarray(L0_attn_w_out, np.float32),
        np.asarray(L0_attn_ln_w, np.float32),
        np.asarray(L0_attn_ln_b, np.float32),
        np.asarray(L0_ffn_W1, np.float32),
        np.asarray(L0_ffn_b1, np.float32),
        np.asarray(L0_ffn_W2, np.float32),
        np.asarray(L0_ffn_b2, np.float32),
        np.asarray(L0_attn_w_in, np.float32))
    in_maps = []
    for b in range(B):
        m = dict(shared)
        m["x"] = np.ascontiguousarray(x[b].T)
        in_maps.append(m)
    res = run_bass_kernel_spmd(nc, in_maps, core_ids=list(range(B)))
    out = np.stack([np.asarray(r["out"]).T for r in res.results], axis=0)
    return np.ascontiguousarray(out.astype(np.float32))
